# version 01
# baseline (speedup 1.0000x reference)
import math
import contextlib

import numpy as np
import ml_dtypes

import concourse.bass as bass
import concourse.mybir as mybir
from concourse.bass_utils import run_bass_kernel_spmd

F32 = mybir.dt.float32
BF16 = mybir.dt.bfloat16
AF = mybir.ActivationFunctionType
ALU = mybir.AluOpType
AX = mybir.AxisListType

DEV = False

B, T, D = 2, 8192, 2048
H, KV, DH = 16, 4, 128
HD = H * DH
KVD = KV * DH
HCH = 44
HID = 5632
OWN = 2048
HALO = 1024
TOT = OWN + HALO
NCH = TOT // 128
OC0 = HALO // 128
QC = 256
NQC = OWN // QC
NKT = (HALO + QC) // 128
KCH = D // 128
TB = 512
NTB = OWN // TB
NTC = TB // 128
SCALE = 1.0 / math.sqrt(DH)
LN_EPS = 1e-5
RMS_EPS = 1e-6
Q_EPS = 1e-8
NEG = -1e30
N_GT = 3 * HCH


def kts_of(qc):
    return list(range(-2 if qc % 2 else 0, 10))


def pairs_in(qc):
    return len(kts_of(qc)) // 2


def gp_base(qc):
    return sum(H * pairs_in(x) for x in range(qc))


def _load_act_set(seng, nc, set_name):
    from concourse.hw_specs import get_activation_tables
    for idx, name in enumerate(get_activation_tables(nc.m.arch).keys()):
        if name == set_name:
            return seng.add_instruction(
                mybir.InstLoadActFuncSet(
                    name=nc.get_next_instruction_name(),
                    act_func_set_id=idx, ins=[], outs=[]))
    raise ValueError(set_name)


def build_program(dev=DEV):
    nc = bass.Bass("TRN2", target_bir_lowering=False)

    xin = nc.dram_tensor("xin", [TOT, D], F32, kind="ExternalInput").ap()
    qkvw = nc.dram_tensor("qkvw", [KCH, 128, HD + 2 * KVD], F32, kind="ExternalInput").ap()
    wow = nc.dram_tensor("wow", [KCH, 128, D], F32, kind="ExternalInput").ap()
    w1w = nc.dram_tensor("w1w", [HCH, 128, KCH, 128], F32, kind="ExternalInput").ap()
    w2w = nc.dram_tensor("w2w", [HCH, 128, KCH, 128], F32, kind="ExternalInput").ap()
    w3w = nc.dram_tensor("w3w", [HCH, 128, D], F32, kind="ExternalInput").ap()
    cosq = nc.dram_tensor("cosq", [TOT, H * 64], BF16, kind="ExternalInput").ap()
    sinq = nc.dram_tensor("sinq", [TOT, H * 64], BF16, kind="ExternalInput").ap()
    tri = nc.dram_tensor("tri", [128, 4, QC], BF16, kind="ExternalInput").ap()
    kbias = nc.dram_tensor("kbias", [128, NCH], F32, kind="ExternalInput").ap()
    idn = nc.dram_tensor("idn", [128, 128], BF16, kind="ExternalInput").ap()

    out = nc.dram_tensor("out", [OWN, D], F32, kind="ExternalOutput").ap()

    skind = "ExternalOutput" if dev else "Internal"
    xns = nc.dram_tensor("xns", [NCH, 128, KCH * 128], BF16, kind=skind).ap()
    ybs = nc.dram_tensor("ybs", [NQC, H, 128, QC], BF16, kind=skind).ap()
    rbs = nc.dram_tensor("rbs", [OWN // 128, 128, D], BF16, kind=skind).ap()
    kdbg = nc.dram_tensor("kdbg", [128, KV, TOT], BF16, kind=skind).ap()
    attdbg = nc.dram_tensor("attdbg", [128, 12, QC], BF16, kind=skind).ap()
    hiddbg = nc.dram_tensor("hiddbg", [128, HCH, TB], BF16, kind=skind).ap()
    rdbdbg = nc.dram_tensor("rdbdbg", [128, QC], BF16, kind=skind).ap()
    vdbg = nc.dram_tensor("vdbg", [128, NCH, KVD], BF16, kind=skind).ap()
    qdbg = nc.dram_tensor("qdbg", [NQC, 128, H, QC], BF16, kind=skind).ap()
    gsc_hbm = nc.dram_tensor("gsc_hbm", [128, N_GT + 4], F32, kind=skind).ap()
    arow_hbm = nc.dram_tensor("arow_hbm", [1, 8], F32, kind=skind).ap()
    qw1s = nc.dram_tensor("qw1s", [HCH, 128, KCH * 128], BF16, kind=skind).ap()
    qw2s = nc.dram_tensor("qw2s", [HCH, 128, KCH * 128], BF16, kind="Internal").ap()
    qw3s = nc.dram_tensor("qw3s", [HCH, 128, D], BF16, kind=skind).ap()

    with contextlib.ExitStack() as top:
        ent = top.enter_context

        idn_sb = ent(nc.sbuf_tensor("idn_sb", [128, 128], BF16))
        tri_sb = ent(nc.sbuf_tensor("tri_sb", [128, 4 * QC], BF16))
        kbias_sb = ent(nc.sbuf_tensor("kbias_sb", [128, NCH], F32))
        ones_c = ent(nc.sbuf_tensor("ones_c", [128, 1], BF16))
        ones_bf = ent(nc.sbuf_tensor("ones_bf", [128, 128], BF16))
        ones_cf = ent(nc.sbuf_tensor("ones_cf", [128, 128], F32))
        gpart = ent(nc.sbuf_tensor("gpart", [128, N_GT + 4], F32))
        a_sb = ent(nc.sbuf_tensor("a_sb", [128, 8], F32))
        sqd = ent(nc.sbuf_tensor("sqd", [128, 128], F32))
        cns = ent(nc.sbuf_tensor("cns", [128, 4], F32))
        dmy = ent(nc.sbuf_tensor("dmy", [128, 2], F32))

        S = {}
        for name in (
            "s0x s0cs s0v s0r s0nb s0n s0t s0e s0w s0ms s1m s1r s1v s1t s1e "
            "sgd sgv sb1 "
            "s2x s2wq s2cs s2m s2sr s2qn s2rp s2tt s2te s2s s2e s2v s2c s2rc "
            "s2b s2bc s2yo s2yw sb2 "
            "s3wl s3y s3m s3e s3r sb3 "
            "s4g s4xn s4w s4q s4sgn s4sg s4h s4ht s4w3l s4w3q s4w3c s4o s4ev "
            "s4r s4x s4out s4qws s4gd sdbg"
        ).split():
            S[name] = ent(nc.semaphore(name))

        block = ent(nc.Block())

        @block.vector
        def _(v):
            v.memset(ones_c[:, :], 1.0)
            v.memset(ones_bf[:, :], 1.0)
            v.memset(ones_cf[:, :], 1.0)
            v.memset(cns[:, 0:1], LN_EPS)
            v.memset(cns[:, 1:2], RMS_EPS)
            v.memset(cns[:, 2:3], 0.5 * Q_EPS)
            v.memset(cns[:, 3:4], -0.5 * Q_EPS)

        @block.sync
        def _(s):
            s.dma_start(out=idn_sb[:, :], in_=idn[:, :]).then_inc(S["s0cs"], 16)
            s.dma_start(out=tri_sb[:, :], in_=tri.rearrange("p f t -> p (f t)")
                        ).then_inc(S["s0cs"], 16)
            s.dma_start(out=kbias_sb[:, :], in_=kbias[:, :]).then_inc(S["s0cs"], 16)

        @block.scalar
        def _(s):
            _load_act_set(s, nc, "natural_log_exp_and_others")

        gtiles = ([(0, hc) for hc in range(HCH)] + [(1, hc) for hc in range(HCH)]
                  + [(2, hc) for hc in range(HCH)])
        gsrc = [w1w, w2w, w3w]

        kv_scope = contextlib.ExitStack()
        k_fm = kv_scope.enter_context(nc.sbuf_tensor("k_fm", [128, KV, TOT], BF16))
        v_tm = kv_scope.enter_context(nc.sbuf_tensor("v_tm", [128, NCH, KVD], BF16))

        with contextlib.ExitStack() as p01:
            e = p01.enter_context
            x_sb = e(nc.sbuf_tensor("x_sb", [128, 2, D], F32))
            mv = e(nc.sbuf_tensor("mv", [128, 2, 2], F32))
            msum = e(nc.sbuf_tensor("msum", [128, 2, 1], F32))
            ssq0 = e(nc.sbuf_tensor("ssq0", [128, 2, 1], F32))
            m2t = e(nc.sbuf_tensor("m2t", [128, 1], F32))
            sqdump = e(nc.sbuf_tensor("sqdump", [128, D], BF16))
            rstd0 = e(nc.sbuf_tensor("rstd0", [128, 2, 1], F32))
            nb0 = e(nc.sbuf_tensor("nb0", [128, 2, 1], F32))
            xn_tm = e(nc.sbuf_tensor("xn_tm", [128, 2, D], BF16))
            xnf = e(nc.sbuf_tensor("xnf", [128, 2, KCH, 128], BF16))
            wkv_bf = e(nc.sbuf_tensor("wkv_bf", [128, KCH, 2 * KVD], BF16))
            cosk = e(nc.sbuf_tensor("cosk", [128, 2, KV * 64], BF16))
            sink = e(nc.sbuf_tensor("sink", [128, 2, KV * 64], BF16))
            kssq = e(nc.sbuf_tensor("kssq", [128, 2, KV], F32))
            krstd = e(nc.sbuf_tensor("krstd", [128, 2, KV], F32))
            kn_bf = e(nc.sbuf_tensor("kn_bf", [128, 2, KV, 128], BF16))
            kr_bf = e(nc.sbuf_tensor("kr_bf", [128, 2, KV, 128], BF16))
            ktmp = e(nc.sbuf_tensor("ktmp", [128, KV, 64], BF16))
            gst = e(nc.sbuf_tensor("gst", [128, D], F32))
            ps_tr = e(nc.psum_tensor("ps_tr", [128, KCH, 128], BF16))
            ps_kv = e(nc.psum_tensor("ps_kv", [128, 2, 2, 512], F32))
            ps_tr1_l = [e(nc.psum_tensor(f"ps_tr1_{i}", [128, KV, 128], BF16)) for i in range(2)]

            n_g01 = 3 * NCH

            @block.sync
            def _(s):
                for c in range(NCH):
                    if c >= 2:
                        s.wait_ge(S["s0n"], c - 1)
                    s.dma_start(out=x_sb[:, c % 2], in_=xin[c * 128:(c + 1) * 128, :]
                                ).then_inc(S["s0x"], 16)
                    if c >= 2:
                        s.wait_ge(S["s1v"], c - 1)
                    s.dma_start(out=cosk[:, c % 2], in_=cosq[c * 128:(c + 1) * 128, 0:KV * 64]
                                ).then_inc(S["s0cs"], 16)
                    s.dma_start(out=sink[:, c % 2], in_=sinq[c * 128:(c + 1) * 128, 0:KV * 64]
                                ).then_inc(S["s0cs"], 16)

            @block.gpsimd
            def _(g):
                for part in range(2):
                    g.dma_start(
                        out=wkv_bf[:, part * 8:(part + 1) * 8, :],
                        in_=qkvw.rearrange("k p f -> p k f")[:, part * 8:(part + 1) * 8, HD:HD + 2 * KVD],
                    ).then_inc(S["s0w"], 16)
                for c in range(NCH):
                    g.wait_ge(S["s0e"], c + 1)
                    g.dma_start(out=xns[c], in_=xnf.rearrange("p s k t -> p s (k t)")[:, c % 2]
                                ).then_inc(S["s0w"], 16)

            @block.scalar
            def _(s):
                for c in range(NCH):
                    for j in range(3):
                        gi = c * 3 + j
                        if gi < n_g01:
                            wsel, hc = gtiles[gi]
                            s.wait_ge(S["sgv"], gi)
                            s.dma_start(out=gst[:, :], in_=gsrc[wsel][hc].rearrange("p a b -> p (a b)")
                                        if wsel < 2 else gsrc[wsel][hc]
                                        ).then_inc(S["sgd"], 16)
                    s.wait_ge(S["s0x"], 16 * (c + 1))
                    if c >= 2:
                        s.wait_ge(S["s0n"], c - 1)
                    s.activation(sqdump[:, :], x_sb[:, c % 2], AF.Square,
                                 accum_out=ssq0[:, c % 2]).then_inc(S["s0nb"], 1)
                    s.wait_ge(S["s0v"], c + 1)
                    s.activation(rstd0[:, c % 2], mv[:, c % 2, 1:2], AF.Ln, bias=cns[:, 0:1])
                    s.activation(rstd0[:, c % 2], rstd0[:, c % 2], AF.Exp, scale=-0.5)
                    s.activation(nb0[:, c % 2], mv[:, c % 2, 0:1], AF.Copy, scale=-1.0
                                 ).then_inc(S["s0r"], 1)
                    s.wait_ge(S["s1m"], c + 1)
                    for h in range(KV):
                        s.activation(sqd[:, :], ps_kv[:, c % 2, 0, h * 128:(h + 1) * 128],
                                     AF.Square, accum_out=kssq[:, c % 2, h:h + 1])
                    s.activation(krstd[:, c % 2], kssq[:, c % 2], AF.Ln,
                                 scale=1.0 / DH, bias=cns[:, 1:2])
                    s.activation(krstd[:, c % 2], krstd[:, c % 2], AF.Exp, scale=-0.5
                                 ).then_inc(S["s1r"], 1)

            @block.vector
            def _(v):
                for c in range(NCH):
                    for j in range(3):
                        gi = c * 3 + j
                        if gi < n_g01:
                            wsel, hc = gtiles[gi]
                            v.wait_ge(S["sgd"], 16 * (gi + 1))
                            v.reduce_sum(gpart[:, gi:gi + 1], gst[:, :], axis=AX.X,
                                         apply_absolute_value=True).then_inc(S["sgv"], 1)
                    v.wait_ge(S["s0x"], 16 * (c + 1))
                    if c >= 2:
                        v.wait_ge(S["s0r"], c - 1)
                        v.wait_ge(S["s0n"], c - 1)
                    v.reduce_sum(msum[:, c % 2], x_sb[:, c % 2], axis=AX.X
                                 ).then_inc(S["s0ms"], 1)
                    v.wait_ge(S["s0ms"], 3 * c + 1)
                    v.tensor_scalar_mul(mv[:, c % 2, 0:1], msum[:, c % 2], 1.0 / D
                                        ).then_inc(S["s0ms"], 1)
                    v.wait_ge(S["s0ms"], 3 * c + 2)
                    v.tensor_mul(m2t[:, :], mv[:, c % 2, 0:1], mv[:, c % 2, 0:1]
                                 ).then_inc(S["s0ms"], 1)
                    v.wait_ge(S["s0ms"], 3 * c + 3)
                    v.wait_ge(S["s0nb"], c + 1)
                    v.scalar_tensor_tensor(mv[:, c % 2, 1:2], ssq0[:, c % 2], 1.0 / D,
                                           m2t[:, :], op0=ALU.mult, op1=ALU.subtract
                                           ).then_inc(S["s0v"], 1)
                    v.wait_ge(S["s0r"], c + 1)
                    v.tensor_scalar(xn_tm[:, c % 2], x_sb[:, c % 2],
                                    scalar1=nb0[:, c % 2], scalar2=rstd0[:, c % 2],
                                    op0=ALU.add, op1=ALU.mult).then_inc(S["s0n"], 1)
                    v.wait_ge(S["s0t"], c + 1)
                    if c >= 2:
                        v.wait_ge(S["s0w"], 16 * (c + 1))
                    v.tensor_copy(xnf[:, c % 2], ps_tr[:, :]).then_inc(S["s0e"], 1)
                    v.wait_ge(S["s1r"], c + 1)
                    v.wait_ge(S["s0cs"], 48 + 32 * (c + 1))
                    if c >= 2:
                        v.wait_ge(S["s1t"], c - 1)
                    for h in range(KV):
                        v.tensor_scalar(kn_bf[:, c % 2, h],
                                        ps_kv[:, c % 2, 0, h * 128:(h + 1) * 128],
                                        scalar1=krstd[:, c % 2, h:h + 1], scalar2=None,
                                        op0=ALU.mult)
                    kn = kn_bf[:, c % 2].rearrange("p h (half j) -> p h half j", half=2)
                    kr = kr_bf[:, c % 2].rearrange("p h (half j) -> p h half j", half=2)
                    ck = cosk[:, c % 2].rearrange("p (h j) -> p h j", h=KV)
                    sk = sink[:, c % 2].rearrange("p (h j) -> p h j", h=KV)
                    v.tensor_mul(ktmp[:, :, :], kn[:, :, 1, :], sk)
                    v.tensor_mul(kr[:, :, 0, :], kn[:, :, 0, :], ck)
                    v.tensor_sub(kr[:, :, 0, :], kr[:, :, 0, :], ktmp[:, :, :])
                    v.tensor_mul(ktmp[:, :, :], kn[:, :, 1, :], ck)
                    v.tensor_mul(kr[:, :, 1, :], kn[:, :, 0, :], sk)
                    v.tensor_add(kr[:, :, 1, :], kr[:, :, 1, :], ktmp[:, :, :])
                    v.tensor_copy(v_tm[:, c, :], ps_kv[:, c % 2, 1, :]).then_inc(S["s1v"], 1)
                    v.wait_ge(S["s1t"], c + 1)
                    v.tensor_copy(k_fm[:, :, c * 128:(c + 1) * 128], ps_tr1_l[c % 2][:, :, :]
                                  ).then_inc(S["s1e"], 1)

            @block.tensor
            def _(t):
                t.wait_ge(S["s0cs"], 16)
                for c in range(NCH):
                    t.wait_ge(S["s0n"], c + 1)
                    if c >= 1:
                        t.wait_ge(S["s0e"], c)
                    for k in range(KCH):
                        ins = t.transpose(ps_tr[:, k],
                                          xn_tm[:, c % 2, k * 128:(k + 1) * 128], idn_sb[:, :])
                    ins.then_inc(S["s0t"], 1)
                    t.wait_ge(S["s0e"], c + 1)
                    t.wait_ge(S["s0w"], 32)
                    if c >= 2:
                        t.wait_ge(S["s1r"], c - 1)
                        t.wait_ge(S["s1v"], c - 1)
                    for k in range(KCH):
                        for half in range(2):
                            ins = t.matmul(ps_kv[:, c % 2, half], xnf[:, c % 2, k],
                                           wkv_bf[:, k, half * 512:(half + 1) * 512],
                                           start=(k == 0), stop=(k == KCH - 1),
                                           skip_group_check=True)
                    ins.then_inc(S["s1m"], 1)
                    t.wait_ge(S["s1v"], c + 1)
                    if c >= 2:
                        t.wait_ge(S["s1e"], c - 1)
                    for h in range(KV):
                        ins = t.transpose(ps_tr1_l[c % 2][:, h], kr_bf[:, c % 2, h], idn_sb[:, :])
                    ins.then_inc(S["s1t"], 1)

            @block.sync
            def _(s):
                s.wait_ge(S["s0w"], 16 * (2 + NCH))
                s.nop().then_inc(S["sb1"], 1)

            @block.gpsimd
            def _(g):
                g.nop().then_inc(S["sb1"], 1)

            @block.scalar
            def _(s):
                s.copy(dmy[:, 0:1], dmy[:, 1:2]).then_inc(S["sb1"], 1)

            @block.vector
            def _(v):
                v.tensor_copy(dmy[:, 0:1], dmy[:, 1:2]).then_inc(S["sb1"], 1)

            @block.tensor
            def _(t):
                t.wait_ge(S["s1e"], NCH)
                t.transpose(ps_tr1_l[0][:, 0], idn_sb[:, :], idn_sb[:, :]
                            ).then_inc(S["sb1"], 1)


        with contextlib.ExitStack() as p2:
            e = p2.enter_context
            wq_bf = e(nc.sbuf_tensor("wq_bf", [128, KCH, HD], BF16))
            xn2 = e(nc.sbuf_tensor("xn2", [128, 2, KCH, 128], BF16))
            cosq2 = e(nc.sbuf_tensor("cosq2", [128, 2, H * 64], BF16))
            sinq2 = e(nc.sbuf_tensor("sinq2", [128, 2, H * 64], BF16))
            qssq = e(nc.sbuf_tensor("qssq", [128, 2, H], F32))
            qrstd = e(nc.sbuf_tensor("qrstd", [128, 2, H], F32))
            qn_bf = e(nc.sbuf_tensor("qn_bf", [128, 2, H, 128], BF16))
            qr_bf = e(nc.sbuf_tensor("qr_bf", [128, 2, H, 128], BF16))
            qtmp = e(nc.sbuf_tensor("qtmp", [128, H, 64], BF16))
            qf_sb = e(nc.sbuf_tensor("qf_sb", [128, 2, H, QC], BF16))
            att_sb = e(nc.sbuf_tensor("att_sb", [128, 2, 12, QC], BF16))
            y_ch = e(nc.sbuf_tensor("y_ch", [128, H, QC], BF16))
            rdb = e(nc.sbuf_tensor("rdb", [128, 2, QC], BF16))
            lntmp = e(nc.sbuf_tensor("lntmp", [128, QC], F32))
            gst2 = e(nc.sbuf_tensor("gst2", [128, D], F32))
            ps_q = e(nc.psum_tensor("ps_q", [128, 2, 512], F32))
            ps_t2_l = [e(nc.psum_tensor(f"ps_t2_{i}", [128, 4, 128], BF16)) for i in range(2)]
            ps_s = e(nc.psum_tensor("ps_s", [128, 2, 512], F32))
            ps_yb = e(nc.psum_tensor("ps_yb", [128, 2, 512], F32))


            @block.sync
            def _(s):
                s.wait_ge(S["sb1"], 5)
                for tc in range(2 * NQC):
                    if tc >= 2:
                        s.wait_ge(S["s2m"], (tc - 2) * 4 + 4)
                    s.dma_start(out=xn2.rearrange("p s k t -> p s (k t)")[:, tc % 2],
                                in_=xns[OC0 + tc]).then_inc(S["s2x"], 16)
                    if tc >= 2:
                        s.wait_ge(S["s2rp"], tc - 1)
                    r0 = HALO + tc * 128
                    s.dma_start(out=cosq2[:, tc % 2], in_=cosq[r0:r0 + 128, :]
                                ).then_inc(S["s2cs"], 16)
                    s.dma_start(out=sinq2[:, tc % 2], in_=sinq[r0:r0 + 128, :]
                                ).then_inc(S["s2cs"], 16)

            @block.gpsimd
            def _(g):
                g.wait_ge(S["sb1"], 5)
                for part in range(4):
                    g.dma_start(
                        out=wq_bf[:, part * 4:(part + 1) * 4, :],
                        in_=qkvw.rearrange("k p f -> p k f")[:, part * 4:(part + 1) * 4, 0:HD],
                    ).then_inc(S["s2wq"], 16)
                if dev:
                    g.wait_ge(S["sb1"], 5)
                    g.dma_start(out=kdbg, in_=k_fm[:, :, :]).then_inc(S["sdbg"], 16)
                    g.dma_start(out=vdbg, in_=v_tm[:, :, :]).then_inc(S["sdbg"], 16)
                for qc in range(NQC):
                    if dev:
                        g.wait_ge(S["s2s"], (qc * H + 1) * 5)
                        g.dma_start(out=qdbg[qc], in_=qf_sb[:, qc % 2, :, :]
                                    ).then_inc(S["sdbg"], 16)
                    g.wait_ge(S["s2yo"], H * (qc + 1))
                    g.dma_start(out=ybs[qc].rearrange("h p t -> p h t"), in_=y_ch[:, :, :]
                                ).then_inc(S["s2yw"], 16)
                if dev:
                    g.wait_ge(S["s2yo"], H * NQC)
                    g.dma_start(out=attdbg, in_=att_sb[:, 1]).then_inc(S["sdbg"], 16)
                    g.dma_start(out=rdbdbg, in_=rdb[:, 1]).then_inc(S["sdbg"], 16)

            @block.scalar
            def _(s):
                s.wait_ge(S["sb1"], 5)
                s.wait_ge(S["s0cs"], 48)
                for qc in range(NQC):
                    for j in range(8):
                        gi = 3 * NCH + qc * 8 + j
                        if gi < N_GT:
                            wsel, hc = gtiles[gi]
                            s.wait_ge(S["sgv"], gi)
                            s.dma_start(out=gst2[:, :],
                                        in_=gsrc[wsel][hc].rearrange("p a b -> p (a b)")
                                        if wsel < 2 else gsrc[wsel][hc]
                                        ).then_inc(S["sgd"], 16)
                    for j in range(2):
                        tc = 2 * qc + j
                        for fb in range(4):
                            nfb = tc * 4 + fb
                            s.wait_ge(S["s2m"], nfb + 1)
                            for h in range(4):
                                s.activation(sqd[:, :],
                                             ps_q[:, nfb % 2, h * 128:(h + 1) * 128],
                                             AF.Square,
                                             accum_out=qssq[:, tc % 2, fb * 4 + h:fb * 4 + h + 1])
                            s.activation(qrstd[:, tc % 2, fb * 4:(fb + 1) * 4],
                                         qssq[:, tc % 2, fb * 4:(fb + 1) * 4],
                                         AF.Ln, scale=1.0 / DH, bias=cns[:, 1:2])
                            s.activation(qrstd[:, tc % 2, fb * 4:(fb + 1) * 4],
                                         qrstd[:, tc % 2, fb * 4:(fb + 1) * 4],
                                         AF.Exp, scale=-0.5).then_inc(S["s2sr"], 1)
                    kts = kts_of(qc)
                    npair = pairs_in(qc)
                    for h in range(H):
                        hh = qc * H + h
                        if hh >= 2:
                            s.wait_ge(S["s2c"], hh - 1)
                        for p in range(npair):
                            gp = gp_base(qc) + h * npair + p
                            s.wait_ge(S["s2s"], gp + 1)
                            if qc <= 3:
                                for kk in range(2):
                                    pos = 2 * p + kk
                                    lkt = 2 * qc + kts[pos]
                                    ins = s.activation(att_sb[:, hh % 2, pos],
                                                       ps_s[:, gp % 2, kk * QC:(kk + 1) * QC],
                                                       AF.Exp, scale=SCALE,
                                                       bias=kbias_sb[:, lkt:lkt + 1])
                                ins.then_inc(S["s2e"], 1)
                            else:
                                s.activation(
                                    att_sb.rearrange("p s k t -> p s (k t)")[:, hh % 2,
                                                                             2 * p * QC:(2 * p + 2) * QC],
                                    ps_s[:, gp % 2, :], AF.Exp, scale=SCALE
                                ).then_inc(S["s2e"], 1)
                        s.wait_ge(S["s2c"], hh + 1)
                        s.activation(lntmp[:, :], ps_yb[:, hh % 2, QC:2 * QC], AF.Ln)
                        s.activation(rdb[:, hh % 2], lntmp[:, :], AF.Exp, scale=-1.0
                                     ).then_inc(S["s2bc"], 1)

            @block.vector
            def _(v):
                v.wait_ge(S["sb1"], 5)
                v.wait_ge(S["s0cs"], 48)
                for qc in range(NQC):
                    for j in range(8):
                        gi = 3 * NCH + qc * 8 + j
                        if gi < N_GT:
                            wsel, hc = gtiles[gi]
                            v.wait_ge(S["sgd"], 16 * (gi + 1))
                            v.reduce_sum(gpart[:, gi:gi + 1], gst2[:, :], axis=AX.X,
                                         apply_absolute_value=True).then_inc(S["sgv"], 1)
                    for j in range(2):
                        tc = 2 * qc + j
                        for fb in range(4):
                            nfb = tc * 4 + fb
                            v.wait_ge(S["s2sr"], nfb + 1)
                            for h in range(4):
                                ins = v.tensor_scalar(qn_bf[:, tc % 2, fb * 4 + h],
                                                      ps_q[:, nfb % 2, h * 128:(h + 1) * 128],
                                                      scalar1=qrstd[:, tc % 2, fb * 4 + h:fb * 4 + h + 1],
                                                      scalar2=None, op0=ALU.mult)
                            ins.then_inc(S["s2qn"], 1)
                        v.wait_ge(S["s2cs"], 32 * (tc + 1))
                        if tc >= 2:
                            v.wait_ge(S["s2tt"], (tc - 1) * 4)
                        qn = qn_bf[:, tc % 2].rearrange("p h (half j) -> p h half j", half=2)
                        qr = qr_bf[:, tc % 2].rearrange("p h (half j) -> p h half j", half=2)
                        cq = cosq2[:, tc % 2].rearrange("p (h j) -> p h j", h=H)
                        sq = sinq2[:, tc % 2].rearrange("p (h j) -> p h j", h=H)
                        v.tensor_mul(qtmp[:, :, :], qn[:, :, 1, :], sq)
                        v.tensor_mul(qr[:, :, 0, :], qn[:, :, 0, :], cq)
                        v.tensor_sub(qr[:, :, 0, :], qr[:, :, 0, :], qtmp[:, :, :])
                        v.tensor_mul(qtmp[:, :, :], qn[:, :, 1, :], cq)
                        v.tensor_mul(qr[:, :, 1, :], qn[:, :, 0, :], sq)
                        v.tensor_add(qr[:, :, 1, :], qr[:, :, 1, :], qtmp[:, :, :]
                                     ).then_inc(S["s2rp"], 1)
                        for grp in range(4):
                            v.wait_ge(S["s2tt"], tc * 4 + grp + 1)
                            v.tensor_copy(
                                qf_sb[:, qc % 2, grp * 4:(grp + 1) * 4, j * 128:(j + 1) * 128],
                                ps_t2_l[(tc * 4 + grp) % 2][:, :, :]).then_inc(S["s2te"], 1)
                    att3 = att_sb.rearrange("p s k t -> p s (k t)")
                    npair = pairs_in(qc)
                    nt = len(kts_of(qc))
                    for h in range(H):
                        hh = qc * H + h
                        gpb = gp_base(qc) + h * npair
                        v.wait_ge(S["s2e"], gpb + 1)
                        v.tensor_mul(att3[:, hh % 2, 0:QC], att3[:, hh % 2, 0:QC],
                                     tri_sb[:, 0:QC]).then_inc(S["s2v"], 1)
                        v.wait_ge(S["s2e"], gpb + npair)
                        v.tensor_mul(att3[:, hh % 2, (nt - 2) * QC:nt * QC],
                                     att3[:, hh % 2, (nt - 2) * QC:nt * QC],
                                     tri_sb[:, 2 * QC:4 * QC]).then_inc(S["s2v"], 1)
                        v.wait_ge(S["s2bc"], hh + 1)
                        if h == 0 and qc >= 1:
                            v.wait_ge(S["s2yw"], 16 * qc)
                        v.tensor_mul(y_ch[:, h, :], ps_yb[:, hh % 2, 0:QC],
                                     rdb[:, hh % 2]).then_inc(S["s2yo"], 1)

            @block.tensor
            def _(t):
                t.wait_ge(S["sb1"], 5)
                t.wait_ge(S["s2wq"], 16 * 4)
                for qc in range(NQC):
                    for j in range(2):
                        tc = 2 * qc + j
                        t.wait_ge(S["s2x"], 16 * (tc + 1))
                        for fb in range(4):
                            nfb = tc * 4 + fb
                            if nfb >= 2:
                                t.wait_ge(S["s2qn"], nfb - 1)
                            for k in range(KCH):
                                ins = t.matmul(ps_q[:, nfb % 2, :], xn2[:, tc % 2, k],
                                               wq_bf[:, k, fb * 512:(fb + 1) * 512],
                                               start=(k == 0), stop=(k == KCH - 1),
                                               skip_group_check=True)
                            ins.then_inc(S["s2m"], 1)
                        t.wait_ge(S["s2rp"], tc + 1)
                        for grp in range(4):
                            ng = tc * 4 + grp
                            if ng >= 2:
                                t.wait_ge(S["s2te"], ng - 1)
                            if qc >= 2 and grp == 0 and j == 0:
                                t.wait_ge(S["s2s"], (qc - 1) * H * 5)
                            for h4 in range(4):
                                ins = t.transpose(ps_t2_l[ng % 2][:, h4],
                                                  qr_bf[:, tc % 2, grp * 4 + h4], idn_sb[:, :])
                            ins.then_inc(S["s2tt"], 1)
                    t.wait_ge(S["s2te"], 8 * qc + 8)
                    kts = kts_of(qc)
                    npair = pairs_in(qc)
                    nt = len(kts)
                    for h in range(H):
                        hh = qc * H + h
                        kvh = h // 4
                        gpb = gp_base(qc) + h * npair
                        for p in range(npair):
                            gp = gpb + p
                            if gp >= 2:
                                t.wait_ge(S["s2e"], gp - 1)
                            for kk in range(2):
                                pos = 2 * p + kk
                                lkt = 2 * qc + kts[pos]
                                ins = t.matmul(ps_s[:, gp % 2, kk * QC:(kk + 1) * QC],
                                               k_fm[:, kvh, lkt * 128:(lkt + 1) * 128],
                                               qf_sb[:, qc % 2, h, :],
                                               start=True, stop=True, skip_group_check=True)
                            ins.then_inc(S["s2s"], 1)
                        if hh >= 2:
                            t.wait_ge(S["s2yo"], hh - 1)
                        for pos in range(nt):
                            p = pos // 2
                            if p == 0 and pos == 0:
                                t.wait_ge(S["s2v"], hh * 2 + 1)
                            elif p == npair - 1:
                                t.wait_ge(S["s2v"], hh * 2 + 2)
                            else:
                                t.wait_ge(S["s2e"], gpb + p + 1)
                            lkt = 2 * qc + kts[pos]
                            t.matmul(ps_yb[:, hh % 2, 0:QC],
                                     v_tm[:, lkt, kvh * 128:(kvh + 1) * 128],
                                     att_sb[:, hh % 2, pos], start=(pos == 0),
                                     stop=(pos == nt - 1), skip_group_check=True)
                        if hh >= 2:
                            t.wait_ge(S["s2bc"], hh - 1)
                        for pos in range(nt):
                            ins = t.matmul(ps_yb[:, hh % 2, QC:2 * QC], ones_bf[:, :],
                                           att_sb[:, hh % 2, pos], start=(pos == 0),
                                           stop=(pos == nt - 1), skip_group_check=True)
                        ins.then_inc(S["s2c"], 1)

            @block.sync
            def _(s):
                s.nop().then_inc(S["sb2"], 1)

            @block.gpsimd
            def _(g):
                g.wait_ge(S["s2yw"], 16 * NQC)
                if dev:
                    g.wait_ge(S["sdbg"], 16 * (4 + NQC))
                g.nop().then_inc(S["sb2"], 1)

            @block.scalar
            def _(s):
                s.copy(dmy[:, 0:1], dmy[:, 1:2]).then_inc(S["sb2"], 1)

            @block.vector
            def _(v):
                v.wait_ge(S["sgv"], N_GT)
                v.tensor_copy(dmy[:, 0:1], dmy[:, 1:2]).then_inc(S["sb2"], 1)

            @block.tensor
            def _(t):
                t.wait_ge(S["s2te"], 8 * NQC)
                t.transpose(ps_t2_l[0][:, 0], idn_sb[:, :], idn_sb[:, :]
                            ).then_inc(S["sb2"], 1)


        kv_scope.close()

        with contextlib.ExitStack() as p3:
            e = p3.enter_context
            wo_bf = e(nc.sbuf_tensor("wo_bf", [128, KCH, D], BF16))
            y3 = e(nc.sbuf_tensor("y3", [128, 2, H, QC], BF16))
            r_bf = e(nc.sbuf_tensor("r_bf", [128, 2, D], BF16))
            ps_o3 = e(nc.psum_tensor("ps_o3", [128, 2, 4, 512], F32))

            @block.gpsimd
            def _(g):
                g.wait_ge(S["sb2"], 5)
                for part in range(4):
                    g.dma_start(out=wo_bf[:, part * 4:(part + 1) * 4, :],
                                in_=wow.rearrange("k p f -> p k f")[:, part * 4:(part + 1) * 4, :]
                                ).then_inc(S["s3wl"], 16)
                for tc in range(OWN // 128):
                    g.wait_ge(S["s3e"], tc + 1)
                    g.dma_start(out=rbs[tc], in_=r_bf[:, tc % 2]).then_inc(S["s3r"], 16)

            @block.sync
            def _(s):
                s.wait_ge(S["sb2"], 5)
                for qc in range(NQC):
                    if qc >= 2:
                        s.wait_ge(S["s3m"], (qc - 1) * 2)
                    s.dma_start(out=y3[:, qc % 2], in_=ybs[qc].rearrange("h p t -> p h t")
                                ).then_inc(S["s3y"], 16)

            @block.tensor
            def _(t):
                t.wait_ge(S["sb2"], 5)
                t.wait_ge(S["s3wl"], 16 * 4)
                for tc in range(OWN // 128):
                    qc, half = tc // 2, tc % 2
                    t.wait_ge(S["s3y"], 16 * (qc + 1))
                    if tc >= 2:
                        t.wait_ge(S["s3e"], tc - 1)
                    for k in range(KCH):
                        for db in range(4):
                            ins = t.matmul(ps_o3[:, tc % 2, db],
                                           y3[:, qc % 2, k, half * 128:(half + 1) * 128],
                                           wo_bf[:, k, db * 512:(db + 1) * 512],
                                           start=(k == 0), stop=(k == KCH - 1),
                                           skip_group_check=True)
                    ins.then_inc(S["s3m"], 1)

            @block.vector
            def _(v):
                v.wait_ge(S["sb2"], 5)
                for tc in range(OWN // 128):
                    v.wait_ge(S["s3m"], tc + 1)
                    if tc >= 2:
                        v.wait_ge(S["s3r"], 16 * (tc - 1))
                    v.tensor_copy(r_bf[:, tc % 2],
                                  ps_o3.rearrange("p s d f -> p s (d f)")[:, tc % 2]
                                  ).then_inc(S["s3e"], 1)

            @block.sync
            def _(s):
                s.nop().then_inc(S["sb3"], 1)

            @block.gpsimd
            def _(g):
                g.wait_ge(S["s3r"], 16 * (OWN // 128))
                g.nop().then_inc(S["sb3"], 1)

            @block.scalar
            def _(s):
                s.wait_ge(S["sb2"], 5)
                _load_act_set(s, nc, "sigmoid_and_others")
                s.copy(dmy[:, 0:1], dmy[:, 1:2]).then_inc(S["sb3"], 1)

            @block.vector
            def _(v):
                v.tensor_copy(dmy[:, 0:1], dmy[:, 1:2]).then_inc(S["sb3"], 1)

            @block.tensor
            def _(t):
                t.wait_ge(S["s3e"], OWN // 128)
                t.matmul(ps_o3[:, 0, 0, 0:128], idn_sb[:, :], idn_sb[:, :],
                         start=True, stop=True, skip_group_check=True
                         ).then_inc(S["sb3"], 1)


        with contextlib.ExitStack() as p4a:
            e = p4a.enter_context
            red4 = e(nc.sbuf_tensor("red4", [1, 3, HCH * 128], F32))
            gsum = e(nc.sbuf_tensor("gsum", [1, 4], F32))
            arow = e(nc.sbuf_tensor("arow", [1, 8], F32))

            @block.sync
            def _(s):
                s.wait_ge(S["sb3"], 5)
                s.dma_start(out=gsc_hbm, in_=gpart[:, :]).then_inc(S["s4gd"], 16)
                s.wait_ge(S["s4gd"], 16)
                for w in range(3):
                    s.dma_start(out=red4[:1, w],
                                in_=gsc_hbm[:, w * HCH:(w + 1) * HCH].rearrange("p t -> t p"),
                                allow_slow_non_contiguous=True).then_inc(S["s4gd"], 16)
                s.wait_ge(S["s4g"], 2)
                s.dma_start(out=arow_hbm[:1, :], in_=arow[:1, :]).then_inc(S["s4gd"], 16)
                s.wait_ge(S["s4gd"], 80)
                s.dma_start(out=a_sb[:, :], in_=arow_hbm[0:1, :].broadcast_to([128, 8])
                            ).then_inc(S["s4gd"], 16)

            @block.vector
            def _(v):
                v.wait_ge(S["s4gd"], 64)
                for w in range(3):
                    ins = v.reduce_sum(gsum[:1, w:w + 1], red4[:1, w], axis=AX.X)
                ins.then_inc(S["s4g"], 1)
                v.wait_ge(S["s4g"], 1)
                NW = float(HID * D)
                for w in range(3):
                    v.tensor_scalar(arow[:1, 2 * w:2 * w + 1], gsum[:1, w:w + 1],
                                    scalar1=0.5 / NW, scalar2=0.5 * Q_EPS,
                                    op0=ALU.mult, op1=ALU.add)
                    ins = v.tensor_scalar(arow[:1, 2 * w + 1:2 * w + 2], gsum[:1, w:w + 1],
                                          scalar1=-0.5 / NW, scalar2=-0.5 * Q_EPS,
                                          op0=ALU.mult, op1=ALU.add)
                ins.then_inc(S["s4g"], 1)

        with contextlib.ExitStack() as p4:
            e = p4.enter_context
            xn4 = e(nc.sbuf_tensor("xn4", [128, NTC, KCH, 128], BF16))
            hid_sb = e(nc.sbuf_tensor("hid_sb", [128, HCH, TB], BF16))
            wst = e(nc.sbuf_tensor("wst", [128, 2, 2, KCH * 128], F32))
            qw12 = e(nc.sbuf_tensor("qw12", [128, 2, 2, KCH, 128], BF16))
            s12 = e(nc.sbuf_tensor("s12", [128, 2, KCH * 128], BF16))
            wst3 = e(nc.sbuf_tensor("wst3", [128, 2, D], F32))
            qw3 = e(nc.sbuf_tensor("qw3", [128, 2, D], BF16))
            s3t = e(nc.sbuf_tensor("s3t", [128, D], BF16))
            sg_bf = e(nc.sbuf_tensor("sg_bf", [128, TB], BF16))
            r4 = e(nc.sbuf_tensor("r4", [128, 2, 1024], BF16))
            x4 = e(nc.sbuf_tensor("x4", [128, 2, 1024], F32))
            tmp4 = e(nc.sbuf_tensor("tmp4", [128, 2, 1024], F32))
            out4 = e(nc.sbuf_tensor("out4", [128, 2, 1024], F32))
            ps_u = e(nc.psum_tensor("ps_u", [128, 4, 1024], F32))

            def psx(slot, w):
                return ps_u[:, slot, w * 512:(w + 1) * 512]

            def qpath(hc):
                return "v" if hc % 3 == 0 else "s"

            def nsgn(hc):
                return sum(1 for x in range(hc + 1) if qpath(x) == "s")

            q12f = lambda sl, w: qw12.rearrange("p s w k t -> p s w (k t)")[:, sl, w]

            @block.scalar
            def _(s):
                s.wait_ge(S["sb3"], 5)
                s.wait_ge(S["s4gd"], 96)
                for b in range(NTB):
                    for hc in range(HCH):
                        ghc = b * HCH + hc
                        for w in range(2):
                            if b == 0:
                                if hc >= 2:
                                    s.wait_ge(S["s4q"], (hc - 2) * 2 + 2)
                                src = w1w if w == 0 else w2w
                                s.dma_start(out=wst[:, hc % 2, w],
                                            in_=src[hc].rearrange("p a b -> p (a b)")
                                            ).then_inc(S["s4w"], 16)
                            else:
                                if ghc >= 2:
                                    s.wait_ge(S["s4ht"], ghc - 1)
                                s.wait_ge(S["s4qws"], 16 * (2 * hc + 2))
                                src = qw1s if w == 0 else qw2s
                                s.dma_start(out=q12f(hc % 2, w), in_=src[hc]
                                            ).then_inc(S["s4w"], 16)
                        if b == 0 and qpath(hc) == "s":
                            s.wait_ge(S["s4w"], 16 * (2 * hc + 2))
                            s.wait_ge(S["s4q"], 2 * hc)
                            if hc >= 2:
                                s.wait_ge(S["s4qws"], 16 * (2 * hc - 2))
                            if hc >= 2:
                                s.wait_ge(S["s4ht"], hc - 1)
                            for w in range(2):
                                s.activation(s12[:, w], wst[:, hc % 2, w], AF.Sign,
                                             bias=a_sb[:, 2 * w + 1:2 * w + 2])
                                ins = s.activation(q12f(hc % 2, w), wst[:, hc % 2, w], AF.Sign,
                                                   bias=a_sb[:, 2 * w:2 * w + 1])
                            ins.then_inc(S["s4sgn"], 1)
                        s.wait_ge(S["s4ht"], ghc + 1)
                        s.wait_ge(S["s4h"], ghc)
                        s.activation(sg_bf[:, :], psx(ghc % 2, 0), AF.Sigmoid
                                     ).then_inc(S["s4sg"], 1)

            @block.vector
            def _(v):
                v.wait_ge(S["sb3"], 5)
                v.wait_ge(S["s4gd"], 96)
                for b in range(NTB):
                    for hc in range(HCH):
                        ghc = b * HCH + hc
                        if b == 0:
                            if qpath(hc) == "v":
                                v.wait_ge(S["s4w"], 16 * (2 * hc + 2))
                                if hc >= 2:
                                    v.wait_ge(S["s4ht"], hc - 1)
                                    v.wait_ge(S["s4qws"], 16 * (2 * hc - 2))
                                for w in range(2):
                                    v.tensor_scalar(s12[:, w], wst[:, hc % 2, w],
                                                    scalar1=a_sb[:, 2 * w:2 * w + 1],
                                                    scalar2=None, op0=ALU.is_gt)
                                    v.tensor_scalar(q12f(hc % 2, w), wst[:, hc % 2, w],
                                                    scalar1=-1.0,
                                                    scalar2=a_sb[:, 2 * w:2 * w + 1],
                                                    op0=ALU.mult, op1=ALU.is_gt)
                                    ins = v.tensor_sub(q12f(hc % 2, w), s12[:, w], q12f(hc % 2, w))
                                ins.then_inc(S["s4q"], 2)
                            else:
                                v.wait_ge(S["s4sgn"], nsgn(hc))
                                if hc >= 2:
                                    v.wait_ge(S["s4ht"], hc - 1)
                                    v.wait_ge(S["s4qws"], 16 * (2 * hc - 2))
                                for w in range(2):
                                    v.tensor_add(q12f(hc % 2, w), q12f(hc % 2, w), s12[:, w])
                                    ins = v.tensor_scalar(q12f(hc % 2, w), q12f(hc % 2, w),
                                                          scalar1=0.5, scalar2=None, op0=ALU.mult)
                                ins.then_inc(S["s4q"], 2)
                        v.wait_ge(S["s4sg"], ghc + 1)
                        v.tensor_mul(hid_sb[:, hc, :], sg_bf[:, :], psx(ghc % 2, 1)
                                     ).then_inc(S["s4h"], 1)
                    for dh in range(2):
                        if b == 0 and dh == 0:
                            for hc in range(HCH):
                                v.wait_ge(S["s4w3l"], 16 * (hc + 1))
                                if hc >= 2:
                                    v.wait_ge(S["s4w3c"], hc - 1)
                                    v.wait_ge(S["s4qws"], 16 * (2 * HCH + hc - 1))
                                v.tensor_scalar(s3t[:, :], wst3[:, hc % 2],
                                                scalar1=a_sb[:, 4:5], scalar2=None,
                                                op0=ALU.is_gt)
                                v.tensor_scalar(qw3[:, hc % 2], wst3[:, hc % 2],
                                                scalar1=-1.0, scalar2=a_sb[:, 4:5],
                                                op0=ALU.mult, op1=ALU.is_gt)
                                v.tensor_sub(qw3[:, hc % 2], s3t[:, :], qw3[:, hc % 2]
                                             ).then_inc(S["s4w3q"], 1)
                        for tc in range(NTC):
                            idx = (b * 2 + dh) * NTC + tc
                            v.wait_ge(S["s4o"], idx + 1)
                            v.wait_ge(S["s4r"], 16 * (idx + 1))
                            v.wait_ge(S["s4x"], 16 * (idx + 1))
                            if idx >= 2:
                                v.wait_ge(S["s4out"], 16 * (idx - 1))
                            sl = idx % 2
                            v.tensor_add(tmp4[:, sl], r4[:, sl], x4[:, sl])
                            v.tensor_add(out4[:, sl], ps_u[:, tc, :], tmp4[:, sl]
                                         ).then_inc(S["s4ev"], 1)

            @block.sync
            def _(s):
                s.wait_ge(S["sb3"], 5)
                for b in range(NTB):
                    if b >= 1:
                        s.wait_ge(S["s4h"], HCH * b)
                    for tc in range(NTC):
                        s.dma_start(out=xn4.rearrange("p f k t -> p f (k t)")[:, tc],
                                    in_=xns[OC0 + b * NTC + tc]).then_inc(S["s4xn"], 16)
                    for dh in range(2):
                        for tc in range(NTC):
                            idx = (b * 2 + dh) * NTC + tc
                            row = b * TB + tc * 128
                            if idx >= 2:
                                s.wait_ge(S["s4ev"], idx - 1)
                            s.dma_start(out=r4[:, idx % 2],
                                        in_=rbs[b * NTC + tc][:, dh * 1024:(dh + 1) * 1024]
                                        ).then_inc(S["s4r"], 16)
                            s.dma_start(out=x4[:, idx % 2],
                                        in_=xin[HALO + row:HALO + row + 128,
                                                dh * 1024:(dh + 1) * 1024]
                                        ).then_inc(S["s4x"], 16)
                            s.wait_ge(S["s4ev"], idx + 1)
                            s.dma_start(out=out[row:row + 128, dh * 1024:(dh + 1) * 1024],
                                        in_=out4[:, idx % 2]).then_inc(S["s4out"], 16)
                s.wait_ge(S["s4out"], 16 * NTB * 2 * NTC)

            @block.gpsimd
            def _(g):
                g.wait_ge(S["sb3"], 5)
                for hc in range(HCH):
                    g.wait_ge(S["s4q"], 2 * hc + 2)
                    g.dma_start(out=qw1s[hc], in_=q12f(hc % 2, 0)).then_inc(S["s4qws"], 16)
                    g.dma_start(out=qw2s[hc], in_=q12f(hc % 2, 1)).then_inc(S["s4qws"], 16)
                for hc in range(HCH):
                    if hc >= 2:
                        g.wait_ge(S["s4w3q"], hc - 1)
                        g.dma_start(out=qw3s[hc - 2], in_=qw3[:, hc % 2]
                                    ).then_inc(S["s4qws"], 16)
                    g.dma_start(out=wst3[:, hc % 2], in_=w3w[hc]).then_inc(S["s4w3l"], 16)
                for hc in (HCH - 2, HCH - 1):
                    g.wait_ge(S["s4w3q"], hc + 1)
                    g.dma_start(out=qw3s[hc], in_=qw3[:, hc % 2]).then_inc(S["s4qws"], 16)
                for si in range(1, 8):
                    for hc in range(HCH):
                        gw3 = si * HCH + hc
                        g.wait_ge(S["s4w3c"], gw3 - 1)
                        g.wait_ge(S["s4qws"], 16 * (2 * HCH + hc + 1))
                        g.dma_start(out=qw3[:, hc % 2], in_=qw3s[hc]).then_inc(S["s4w3l"], 16)
                if dev:
                    g.wait_ge(S["s4h"], HCH * NTB)
                    g.dma_start(out=hiddbg, in_=hid_sb[:, :, :]).then_inc(S["sdbg"], 16)

            @block.tensor
            def _(t):
                t.wait_ge(S["sb3"], 5)
                t.wait_ge(S["s4gd"], 96)
                for b in range(NTB):
                    t.wait_ge(S["s4xn"], 16 * NTC * (b + 1))
                    if b >= 1:
                        t.wait_ge(S["s4ev"], b * 2 * NTC)
                    for hc in range(HCH):
                        ghc = b * HCH + hc
                        if b == 0:
                            t.wait_ge(S["s4q"], 2 * hc + 2)
                        else:
                            t.wait_ge(S["s4w"], 16 * (2 * ghc + 2))
                        if ghc >= 2:
                            t.wait_ge(S["s4sg"], ghc - 1)
                            t.wait_ge(S["s4h"], ghc - 1)
                        for w in range(2):
                            for k in range(KCH):
                                ins = t.matmul(psx(ghc % 2, w), qw12[:, hc % 2, w, k],
                                               xn4[:, :, k, :],
                                               start=(k == 0), stop=(k == KCH - 1),
                                               skip_group_check=True)
                        ins.then_inc(S["s4ht"], 1)
                    for dh in range(2):
                        si = b * 2 + dh
                        for hc in range(HCH):
                            gw3 = si * HCH + hc
                            if si == 0:
                                t.wait_ge(S["s4w3q"], hc + 1)
                            else:
                                t.wait_ge(S["s4w3l"], 16 * (gw3 + 1))
                            if hc == 0:
                                t.wait_ge(S["s4h"], HCH * (b + 1))
                                if si >= 1:
                                    t.wait_ge(S["s4ev"], si * NTC)
                            for tc in range(NTC):
                                for db in range(2):
                                    ins = t.matmul(
                                        ps_u[:, tc, db * 512:(db + 1) * 512],
                                        hid_sb[:, hc, tc * 128:(tc + 1) * 128],
                                        qw3[:, hc % 2, dh * 1024 + db * 512:dh * 1024 + (db + 1) * 512],
                                        start=(hc == 0), stop=(hc == HCH - 1),
                                        skip_group_check=True)
                                if hc == HCH - 1 and tc < NTC - 1:
                                    ins.then_inc(S["s4o"], 1)
                            if hc == HCH - 1:
                                ins.then_inc(S["s4o"], 1)
                                t.wait_ge(S["s4o"], (b * 2 + dh) * NTC + NTC)
                                t.nop().then_inc(S["s4w3c"], 1)
                            else:
                                ins.then_inc(S["s4w3c"], 1)

    return nc


_CACHE = {}


def _host_prep(L0_attn_x, L0_attn_wq, L0_attn_wk, L0_attn_wv, L0_attn_wo,
               L0_ffn_w1, L0_ffn_w2, L0_ffn_w3):
    bf = ml_dtypes.bfloat16
    X2 = np.ascontiguousarray(np.asarray(L0_attn_x, dtype=np.float32).reshape(B * T, D))

    perm = np.concatenate([h * DH + np.concatenate([np.arange(0, DH, 2), np.arange(1, DH, 2)])
                           for h in range(H)])
    permk = np.concatenate([h * DH + np.concatenate([np.arange(0, DH, 2), np.arange(1, DH, 2)])
                            for h in range(KV)])
    wq = np.asarray(L0_attn_wq, np.float32)[perm]
    wk = np.asarray(L0_attn_wk, np.float32)[permk]
    wv = np.asarray(L0_attn_wv, np.float32)
    qkv = np.ascontiguousarray(
        np.concatenate([wq, wk, wv], axis=0).T.reshape(KCH, 128, HD + 2 * KVD))
    wow_np = np.ascontiguousarray(np.asarray(L0_attn_wo, np.float32).T.reshape(KCH, 128, D))
    w1w_np = np.ascontiguousarray(
        np.asarray(L0_ffn_w1, np.float32).reshape(HCH, 128, KCH, 128).transpose(0, 3, 2, 1))
    w2w_np = np.ascontiguousarray(
        np.asarray(L0_ffn_w2, np.float32).reshape(HCH, 128, KCH, 128).transpose(0, 3, 2, 1))
    w3w_np = np.ascontiguousarray(np.asarray(L0_ffn_w3, np.float32).T.reshape(HCH, 128, D))

    inv = (1.0 / 10000.0 ** (np.arange(0, DH, 2, dtype=np.float64) / DH)).astype(np.float32)
    kk = np.arange(128)[:, None]
    qq = np.arange(QC)[None, :]
    tri_np = np.zeros((128, 4, QC), dtype=bf)
    tri_np[:, 0] = ((kk >= 1) | (qq < 0)).astype(bf)
    tri_np[:, 1] = 1.0
    tri_np[:, 2] = (kk <= qq).astype(bf)
    tri_np[:, 3] = (128 + kk <= qq).astype(bf)
    idn_np = np.eye(128, dtype=bf)

    in_maps = []
    for c in range(8):
        seq, slot = divmod(c, 4)
        pos0 = slot * OWN
        r0 = seq * T + pos0
        ownx = X2[r0:r0 + OWN]
        halo = np.zeros((HALO, D), np.float32) if pos0 == 0 else X2[r0 - HALO:r0]
        xin_np = np.ascontiguousarray(np.concatenate([halo, ownx], axis=0))

        pos = np.maximum(pos0 - HALO + np.arange(TOT), 0).astype(np.float32)
        ang = (pos[:, None] * inv[None, :]).astype(np.float32)
        cos_t = np.cos(ang.astype(np.float64)).astype(np.float32)
        sin_t = np.sin(ang.astype(np.float64)).astype(np.float32)
        cosq_np = np.ascontiguousarray(np.tile(cos_t, (1, H)).astype(bf))
        sinq_np = np.ascontiguousarray(np.tile(sin_t, (1, H)).astype(bf))

        kb = np.zeros((128, NCH), np.float32)
        gkey = pos0 - HALO + (np.arange(NCH)[None, :] * 128 + np.arange(128)[:, None])
        kb[gkey < 0] = NEG

        in_maps.append({
            "xin": xin_np, "qkvw": qkv, "wow": wow_np, "w1w": w1w_np,
            "w2w": w2w_np, "w3w": w3w_np, "cosq": cosq_np, "sinq": sinq_np,
            "tri": tri_np, "kbias": kb, "idn": idn_np,
        })
    return in_maps


def kernel(**inputs):
    in_maps = _host_prep(**inputs)
    if "nc" not in _CACHE:
        _CACHE["nc"] = build_program()
    res = run_bass_kernel_spmd(_CACHE["nc"], in_maps, core_ids=list(range(8)))
    Y = np.empty((B * T, D), dtype=np.float32)
    for c in range(8):
        seq, slot = divmod(c, 4)
        r0 = seq * T + slot * OWN
        Y[r0:r0 + OWN] = res.results[c]["out"]
    return Y.reshape(B, T, D)


# version 02 (final)
# speedup vs baseline: 1.0035x; 1.0035x over previous
import math
import contextlib

import numpy as np
import ml_dtypes

import concourse.bass as bass
import concourse.mybir as mybir
from concourse.bass_utils import run_bass_kernel_spmd

F32 = mybir.dt.float32
BF16 = mybir.dt.bfloat16
AF = mybir.ActivationFunctionType
ALU = mybir.AluOpType
AX = mybir.AxisListType

DEV = False

B, T, D = 2, 8192, 2048
H, KV, DH = 16, 4, 128
HD = H * DH
KVD = KV * DH
HCH = 44
HID = 5632
OWN = 2048
HALO = 1024
TOT = OWN + HALO
NCH = TOT // 128
OC0 = HALO // 128
QC = 256
NQC = OWN // QC
NKT = (HALO + QC) // 128
KCH = D // 128
TB = 512
NTB = OWN // TB
NTC = TB // 128
SCALE = 1.0 / math.sqrt(DH)
LN_EPS = 1e-5
RMS_EPS = 1e-6
Q_EPS = 1e-8
NEG = -1e30
N_GT = 3 * HCH


def kts_of(qc):
    return list(range(-2 if qc % 2 else 0, 10))


def pairs_in(qc):
    return len(kts_of(qc)) // 2


def gp_base(qc):
    return sum(H * pairs_in(x) for x in range(qc))


def _load_act_set(seng, nc, set_name):
    from concourse.hw_specs import get_activation_tables
    for idx, name in enumerate(get_activation_tables(nc.m.arch).keys()):
        if name == set_name:
            return seng.add_instruction(
                mybir.InstLoadActFuncSet(
                    name=nc.get_next_instruction_name(),
                    act_func_set_id=idx, ins=[], outs=[]))
    raise ValueError(set_name)


def build_program(dev=DEV):
    nc = bass.Bass("TRN2", target_bir_lowering=False)

    xin = nc.dram_tensor("xin", [TOT, D], F32, kind="ExternalInput").ap()
    qkvw = nc.dram_tensor("qkvw", [KCH, 128, HD + 2 * KVD], F32, kind="ExternalInput").ap()
    wow = nc.dram_tensor("wow", [KCH, 128, D], F32, kind="ExternalInput").ap()
    w1w = nc.dram_tensor("w1w", [HCH, 128, KCH, 128], F32, kind="ExternalInput").ap()
    w2w = nc.dram_tensor("w2w", [HCH, 128, KCH, 128], F32, kind="ExternalInput").ap()
    w3w = nc.dram_tensor("w3w", [HCH, 128, D], F32, kind="ExternalInput").ap()
    cosq = nc.dram_tensor("cosq", [TOT, H * 64], BF16, kind="ExternalInput").ap()
    sinq = nc.dram_tensor("sinq", [TOT, H * 64], BF16, kind="ExternalInput").ap()
    tri = nc.dram_tensor("tri", [128, 4, QC], BF16, kind="ExternalInput").ap()
    kbias = nc.dram_tensor("kbias", [128, NCH], F32, kind="ExternalInput").ap()
    idn = nc.dram_tensor("idn", [128, 128], BF16, kind="ExternalInput").ap()

    out = nc.dram_tensor("out", [OWN, D], F32, kind="ExternalOutput").ap()

    skind = "ExternalOutput" if dev else "Internal"
    xns = nc.dram_tensor("xns", [NCH, 128, KCH * 128], BF16, kind=skind).ap()
    ybs = nc.dram_tensor("ybs", [NQC, H, 128, QC], BF16, kind=skind).ap()
    rbs = nc.dram_tensor("rbs", [OWN // 128, 128, D], BF16, kind=skind).ap()
    kdbg = nc.dram_tensor("kdbg", [128, KV, TOT], BF16, kind=skind).ap()
    attdbg = nc.dram_tensor("attdbg", [128, 12, QC], BF16, kind=skind).ap()
    hiddbg = nc.dram_tensor("hiddbg", [128, HCH, TB], BF16, kind=skind).ap()
    rdbdbg = nc.dram_tensor("rdbdbg", [128, QC], BF16, kind=skind).ap()
    vdbg = nc.dram_tensor("vdbg", [128, NCH, KVD], BF16, kind=skind).ap()
    qdbg = nc.dram_tensor("qdbg", [NQC, 128, H, QC], BF16, kind=skind).ap()
    gsc_hbm = nc.dram_tensor("gsc_hbm", [128, N_GT + 4], F32, kind=skind).ap()
    arow_hbm = nc.dram_tensor("arow_hbm", [1, 8], F32, kind=skind).ap()
    qw1s = nc.dram_tensor("qw1s", [HCH, 128, KCH * 128], BF16, kind=skind).ap()
    qw2s = nc.dram_tensor("qw2s", [HCH, 128, KCH * 128], BF16, kind="Internal").ap()
    qw3s = nc.dram_tensor("qw3s", [HCH, 128, D], BF16, kind=skind).ap()

    with contextlib.ExitStack() as top:
        ent = top.enter_context

        idn_sb = ent(nc.sbuf_tensor("idn_sb", [128, 128], BF16))
        tri_sb = ent(nc.sbuf_tensor("tri_sb", [128, 4 * QC], BF16))
        kbias_sb = ent(nc.sbuf_tensor("kbias_sb", [128, NCH], F32))
        ones_c = ent(nc.sbuf_tensor("ones_c", [128, 1], BF16))
        ones_bf = ent(nc.sbuf_tensor("ones_bf", [128, 128], BF16))
        ones_cf = ent(nc.sbuf_tensor("ones_cf", [128, 128], F32))
        gpart = ent(nc.sbuf_tensor("gpart", [128, N_GT + 4], F32))
        a_sb = ent(nc.sbuf_tensor("a_sb", [128, 8], F32))
        sqd = ent(nc.sbuf_tensor("sqd", [128, 128], F32))
        cns = ent(nc.sbuf_tensor("cns", [128, 4], F32))
        dmy = ent(nc.sbuf_tensor("dmy", [128, 2], F32))

        S = {}
        for name in (
            "s0x s0cs s0v s0r s0nb s0n s0t s0e s0w s0ms s1m s1r s1v s1t s1e "
            "sgd sgv sb1 "
            "s2x s2wq s2cs s2m s2sr s2qn s2rp s2tt s2te s2s s2e s2v s2c s2rc "
            "s2b s2bc s2yo s2yw sb2 "
            "s3wl s3y s3m s3e s3r sb3 "
            "s4g s4xn s4w s4q s4sgn s4sg s4h s4ht s4w3l s4w3q s4w3c s4o s4ev "
            "s4r s4x s4out s4qws s4gd s4tt sdbg"
        ).split():
            S[name] = ent(nc.semaphore(name))

        block = ent(nc.Block())

        @block.vector
        def _(v):
            v.memset(ones_c[:, :], 1.0)
            v.memset(ones_bf[:, :], 1.0)
            v.memset(ones_cf[:, :], 1.0)
            v.memset(cns[:, 0:1], LN_EPS)
            v.memset(cns[:, 1:2], RMS_EPS)
            v.memset(cns[:, 2:3], 0.5 * Q_EPS)
            v.memset(cns[:, 3:4], -0.5 * Q_EPS)

        @block.sync
        def _(s):
            s.dma_start(out=idn_sb[:, :], in_=idn[:, :]).then_inc(S["s0cs"], 16)
            s.dma_start(out=tri_sb[:, :], in_=tri.rearrange("p f t -> p (f t)")
                        ).then_inc(S["s0cs"], 16)
            s.dma_start(out=kbias_sb[:, :], in_=kbias[:, :]).then_inc(S["s0cs"], 16)

        @block.scalar
        def _(s):
            _load_act_set(s, nc, "natural_log_exp_and_others")

        gtiles = ([(0, hc) for hc in range(HCH)] + [(1, hc) for hc in range(HCH)]
                  + [(2, hc) for hc in range(HCH)])
        gsrc = [w1w, w2w, w3w]

        kv_scope = contextlib.ExitStack()
        k_fm = kv_scope.enter_context(nc.sbuf_tensor("k_fm", [128, KV, TOT], BF16))
        v_tm = kv_scope.enter_context(nc.sbuf_tensor("v_tm", [128, NCH, KVD], BF16))

        with contextlib.ExitStack() as p01:
            e = p01.enter_context
            x_sb = e(nc.sbuf_tensor("x_sb", [128, 2, D], F32))
            mv = e(nc.sbuf_tensor("mv", [128, 2, 2], F32))
            msum = e(nc.sbuf_tensor("msum", [128, 2, 1], F32))
            ssq0 = e(nc.sbuf_tensor("ssq0", [128, 2, 1], F32))
            m2t = e(nc.sbuf_tensor("m2t", [128, 1], F32))
            sqdump = e(nc.sbuf_tensor("sqdump", [128, D], BF16))
            rstd0 = e(nc.sbuf_tensor("rstd0", [128, 2, 1], F32))
            nb0 = e(nc.sbuf_tensor("nb0", [128, 2, 1], F32))
            xn_tm = e(nc.sbuf_tensor("xn_tm", [128, 2, D], BF16))
            xnf = e(nc.sbuf_tensor("xnf", [128, 2, KCH, 128], BF16))
            wkv_bf = e(nc.sbuf_tensor("wkv_bf", [128, KCH, 2 * KVD], BF16))
            cosk = e(nc.sbuf_tensor("cosk", [128, 2, KV * 64], BF16))
            sink = e(nc.sbuf_tensor("sink", [128, 2, KV * 64], BF16))
            kssq = e(nc.sbuf_tensor("kssq", [128, 2, KV], F32))
            krstd = e(nc.sbuf_tensor("krstd", [128, 2, KV], F32))
            kn_bf = e(nc.sbuf_tensor("kn_bf", [128, 2, KV, 128], BF16))
            kr_bf = e(nc.sbuf_tensor("kr_bf", [128, 2, KV, 128], BF16))
            ktmp = e(nc.sbuf_tensor("ktmp", [128, KV, 64], BF16))
            gst = e(nc.sbuf_tensor("gst", [128, D], F32))
            ps_tr = e(nc.psum_tensor("ps_tr", [128, KCH, 128], BF16))
            ps_kv = e(nc.psum_tensor("ps_kv", [128, 2, 2, 512], F32))
            ps_tr1_l = [e(nc.psum_tensor(f"ps_tr1_{i}", [128, KV, 128], BF16)) for i in range(2)]

            n_g01 = 3 * NCH

            @block.sync
            def _(s):
                for c in range(NCH):
                    if c >= 2:
                        s.wait_ge(S["s0n"], c - 1)
                    s.dma_start(out=x_sb[:, c % 2], in_=xin[c * 128:(c + 1) * 128, :]
                                ).then_inc(S["s0x"], 16)
                    if c >= 2:
                        s.wait_ge(S["s1v"], c - 1)
                    s.dma_start(out=cosk[:, c % 2], in_=cosq[c * 128:(c + 1) * 128, 0:KV * 64]
                                ).then_inc(S["s0cs"], 16)
                    s.dma_start(out=sink[:, c % 2], in_=sinq[c * 128:(c + 1) * 128, 0:KV * 64]
                                ).then_inc(S["s0cs"], 16)

            @block.gpsimd
            def _(g):
                for part in range(2):
                    g.dma_start(
                        out=wkv_bf[:, part * 8:(part + 1) * 8, :],
                        in_=qkvw.rearrange("k p f -> p k f")[:, part * 8:(part + 1) * 8, HD:HD + 2 * KVD],
                    ).then_inc(S["s0w"], 16)
                for c in range(NCH):
                    g.wait_ge(S["s0e"], c + 1)
                    g.dma_start(out=xns[c], in_=xnf.rearrange("p s k t -> p s (k t)")[:, c % 2]
                                ).then_inc(S["s0w"], 16)

            @block.scalar
            def _(s):
                for c in range(NCH):
                    for j in range(3):
                        gi = c * 3 + j
                        if gi < n_g01:
                            wsel, hc = gtiles[gi]
                            s.wait_ge(S["sgv"], gi)
                            s.dma_start(out=gst[:, :], in_=gsrc[wsel][hc].rearrange("p a b -> p (a b)")
                                        if wsel < 2 else gsrc[wsel][hc]
                                        ).then_inc(S["sgd"], 16)
                    s.wait_ge(S["s0x"], 16 * (c + 1))
                    if c >= 2:
                        s.wait_ge(S["s0n"], c - 1)
                    s.activation(sqdump[:, :], x_sb[:, c % 2], AF.Square,
                                 accum_out=ssq0[:, c % 2]).then_inc(S["s0nb"], 1)
                    s.wait_ge(S["s0v"], c + 1)
                    s.activation(rstd0[:, c % 2], mv[:, c % 2, 1:2], AF.Ln, bias=cns[:, 0:1])
                    s.activation(rstd0[:, c % 2], rstd0[:, c % 2], AF.Exp, scale=-0.5)
                    s.activation(nb0[:, c % 2], mv[:, c % 2, 0:1], AF.Copy, scale=-1.0
                                 ).then_inc(S["s0r"], 1)
                    s.wait_ge(S["s1m"], c + 1)
                    for h in range(KV):
                        s.activation(sqd[:, :], ps_kv[:, c % 2, 0, h * 128:(h + 1) * 128],
                                     AF.Square, accum_out=kssq[:, c % 2, h:h + 1])
                    s.activation(krstd[:, c % 2], kssq[:, c % 2], AF.Ln,
                                 scale=1.0 / DH, bias=cns[:, 1:2])
                    s.activation(krstd[:, c % 2], krstd[:, c % 2], AF.Exp, scale=-0.5
                                 ).then_inc(S["s1r"], 1)

            @block.vector
            def _(v):
                for c in range(NCH):
                    for j in range(3):
                        gi = c * 3 + j
                        if gi < n_g01:
                            wsel, hc = gtiles[gi]
                            v.wait_ge(S["sgd"], 16 * (gi + 1))
                            v.reduce_sum(gpart[:, gi:gi + 1], gst[:, :], axis=AX.X,
                                         apply_absolute_value=True).then_inc(S["sgv"], 1)
                    v.wait_ge(S["s0x"], 16 * (c + 1))
                    if c >= 2:
                        v.wait_ge(S["s0r"], c - 1)
                        v.wait_ge(S["s0n"], c - 1)
                    v.reduce_sum(msum[:, c % 2], x_sb[:, c % 2], axis=AX.X
                                 ).then_inc(S["s0ms"], 1)
                    v.wait_ge(S["s0ms"], 3 * c + 1)
                    v.tensor_scalar_mul(mv[:, c % 2, 0:1], msum[:, c % 2], 1.0 / D
                                        ).then_inc(S["s0ms"], 1)
                    v.wait_ge(S["s0ms"], 3 * c + 2)
                    v.tensor_mul(m2t[:, :], mv[:, c % 2, 0:1], mv[:, c % 2, 0:1]
                                 ).then_inc(S["s0ms"], 1)
                    v.wait_ge(S["s0ms"], 3 * c + 3)
                    v.wait_ge(S["s0nb"], c + 1)
                    v.scalar_tensor_tensor(mv[:, c % 2, 1:2], ssq0[:, c % 2], 1.0 / D,
                                           m2t[:, :], op0=ALU.mult, op1=ALU.subtract
                                           ).then_inc(S["s0v"], 1)
                    v.wait_ge(S["s0r"], c + 1)
                    v.tensor_scalar(xn_tm[:, c % 2], x_sb[:, c % 2],
                                    scalar1=nb0[:, c % 2], scalar2=rstd0[:, c % 2],
                                    op0=ALU.add, op1=ALU.mult).then_inc(S["s0n"], 1)
                    v.wait_ge(S["s0t"], c + 1)
                    if c >= 2:
                        v.wait_ge(S["s0w"], 16 * (c + 1))
                    v.tensor_copy(xnf[:, c % 2], ps_tr[:, :]).then_inc(S["s0e"], 1)
                    v.wait_ge(S["s1r"], c + 1)
                    v.wait_ge(S["s0cs"], 48 + 32 * (c + 1))
                    if c >= 2:
                        v.wait_ge(S["s1t"], c - 1)
                    for h in range(KV):
                        v.tensor_scalar(kn_bf[:, c % 2, h],
                                        ps_kv[:, c % 2, 0, h * 128:(h + 1) * 128],
                                        scalar1=krstd[:, c % 2, h:h + 1], scalar2=None,
                                        op0=ALU.mult)
                    kn = kn_bf[:, c % 2].rearrange("p h (half j) -> p h half j", half=2)
                    kr = kr_bf[:, c % 2].rearrange("p h (half j) -> p h half j", half=2)
                    ck = cosk[:, c % 2].rearrange("p (h j) -> p h j", h=KV)
                    sk = sink[:, c % 2].rearrange("p (h j) -> p h j", h=KV)
                    v.tensor_mul(ktmp[:, :, :], kn[:, :, 1, :], sk)
                    v.tensor_mul(kr[:, :, 0, :], kn[:, :, 0, :], ck)
                    v.tensor_sub(kr[:, :, 0, :], kr[:, :, 0, :], ktmp[:, :, :])
                    v.tensor_mul(ktmp[:, :, :], kn[:, :, 1, :], ck)
                    v.tensor_mul(kr[:, :, 1, :], kn[:, :, 0, :], sk)
                    v.tensor_add(kr[:, :, 1, :], kr[:, :, 1, :], ktmp[:, :, :])
                    v.tensor_copy(v_tm[:, c, :], ps_kv[:, c % 2, 1, :]).then_inc(S["s1v"], 1)
                    v.wait_ge(S["s1t"], c + 1)
                    v.tensor_copy(k_fm[:, :, c * 128:(c + 1) * 128], ps_tr1_l[c % 2][:, :, :]
                                  ).then_inc(S["s1e"], 1)

            @block.tensor
            def _(t):
                t.wait_ge(S["s0cs"], 16)
                for c in range(NCH):
                    t.wait_ge(S["s0n"], c + 1)
                    if c >= 1:
                        t.wait_ge(S["s0e"], c)
                    for k in range(KCH):
                        ins = t.transpose(ps_tr[:, k],
                                          xn_tm[:, c % 2, k * 128:(k + 1) * 128], idn_sb[:, :])
                    ins.then_inc(S["s0t"], 1)
                    t.wait_ge(S["s0e"], c + 1)
                    t.wait_ge(S["s0w"], 32)
                    if c >= 2:
                        t.wait_ge(S["s1r"], c - 1)
                        t.wait_ge(S["s1v"], c - 1)
                    for k in range(KCH):
                        for half in range(2):
                            ins = t.matmul(ps_kv[:, c % 2, half], xnf[:, c % 2, k],
                                           wkv_bf[:, k, half * 512:(half + 1) * 512],
                                           start=(k == 0), stop=(k == KCH - 1),
                                           skip_group_check=True)
                    ins.then_inc(S["s1m"], 1)
                    t.wait_ge(S["s1v"], c + 1)
                    if c >= 2:
                        t.wait_ge(S["s1e"], c - 1)
                    for h in range(KV):
                        ins = t.transpose(ps_tr1_l[c % 2][:, h], kr_bf[:, c % 2, h], idn_sb[:, :])
                    ins.then_inc(S["s1t"], 1)

            @block.sync
            def _(s):
                s.wait_ge(S["s0w"], 16 * (2 + NCH))
                s.nop().then_inc(S["sb1"], 1)

            @block.gpsimd
            def _(g):
                g.nop().then_inc(S["sb1"], 1)

            @block.scalar
            def _(s):
                s.copy(dmy[:, 0:1], dmy[:, 1:2]).then_inc(S["sb1"], 1)

            @block.vector
            def _(v):
                v.tensor_copy(dmy[:, 0:1], dmy[:, 1:2]).then_inc(S["sb1"], 1)

            @block.tensor
            def _(t):
                t.wait_ge(S["s1e"], NCH)
                t.transpose(ps_tr1_l[0][:, 0], idn_sb[:, :], idn_sb[:, :]
                            ).then_inc(S["sb1"], 1)


        with contextlib.ExitStack() as p2:
            e = p2.enter_context
            wq_bf = e(nc.sbuf_tensor("wq_bf", [128, KCH, HD], BF16))
            xn2 = e(nc.sbuf_tensor("xn2", [128, 2, KCH, 128], BF16))
            cosq2 = e(nc.sbuf_tensor("cosq2", [128, 2, H * 64], BF16))
            sinq2 = e(nc.sbuf_tensor("sinq2", [128, 2, H * 64], BF16))
            qssq = e(nc.sbuf_tensor("qssq", [128, 2, H], F32))
            qrstd = e(nc.sbuf_tensor("qrstd", [128, 2, H], F32))
            qn_bf = e(nc.sbuf_tensor("qn_bf", [128, 2, H, 128], BF16))
            qr_bf = e(nc.sbuf_tensor("qr_bf", [128, 2, H, 128], BF16))
            qtmp = e(nc.sbuf_tensor("qtmp", [128, H, 64], BF16))
            qf_sb = e(nc.sbuf_tensor("qf_sb", [128, 2, H, QC], BF16))
            att_sb = e(nc.sbuf_tensor("att_sb", [128, 2, 12, QC], BF16))
            y_ch = e(nc.sbuf_tensor("y_ch", [128, H, QC], BF16))
            rdb = e(nc.sbuf_tensor("rdb", [128, 2, QC], BF16))
            lntmp = e(nc.sbuf_tensor("lntmp", [128, QC], F32))
            gst2 = e(nc.sbuf_tensor("gst2", [128, D], F32))
            ps_q = e(nc.psum_tensor("ps_q", [128, 2, 512], F32))
            ps_t2_l = [e(nc.psum_tensor(f"ps_t2_{i}", [128, 4, 128], BF16)) for i in range(2)]
            ps_s = e(nc.psum_tensor("ps_s", [128, 2, 512], F32))
            ps_yb = e(nc.psum_tensor("ps_yb", [128, 2, 512], F32))


            @block.sync
            def _(s):
                s.wait_ge(S["sb1"], 5)
                for tc in range(2 * NQC):
                    if tc >= 2:
                        s.wait_ge(S["s2m"], (tc - 2) * 4 + 4)
                    s.dma_start(out=xn2.rearrange("p s k t -> p s (k t)")[:, tc % 2],
                                in_=xns[OC0 + tc]).then_inc(S["s2x"], 16)
                    if tc >= 2:
                        s.wait_ge(S["s2rp"], tc - 1)
                    r0 = HALO + tc * 128
                    s.dma_start(out=cosq2[:, tc % 2], in_=cosq[r0:r0 + 128, :]
                                ).then_inc(S["s2cs"], 16)
                    s.dma_start(out=sinq2[:, tc % 2], in_=sinq[r0:r0 + 128, :]
                                ).then_inc(S["s2cs"], 16)

            @block.gpsimd
            def _(g):
                g.wait_ge(S["sb1"], 5)
                for part in range(4):
                    g.dma_start(
                        out=wq_bf[:, part * 4:(part + 1) * 4, :],
                        in_=qkvw.rearrange("k p f -> p k f")[:, part * 4:(part + 1) * 4, 0:HD],
                    ).then_inc(S["s2wq"], 16)
                if dev:
                    g.wait_ge(S["sb1"], 5)
                    g.dma_start(out=kdbg, in_=k_fm[:, :, :]).then_inc(S["sdbg"], 16)
                    g.dma_start(out=vdbg, in_=v_tm[:, :, :]).then_inc(S["sdbg"], 16)
                for qc in range(NQC):
                    if dev:
                        g.wait_ge(S["s2s"], (qc * H + 1) * 5)
                        g.dma_start(out=qdbg[qc], in_=qf_sb[:, qc % 2, :, :]
                                    ).then_inc(S["sdbg"], 16)
                    g.wait_ge(S["s2yo"], H * (qc + 1))
                    g.dma_start(out=ybs[qc].rearrange("h p t -> p h t"), in_=y_ch[:, :, :]
                                ).then_inc(S["s2yw"], 16)
                if dev:
                    g.wait_ge(S["s2yo"], H * NQC)
                    g.dma_start(out=attdbg, in_=att_sb[:, 1]).then_inc(S["sdbg"], 16)
                    g.dma_start(out=rdbdbg, in_=rdb[:, 1]).then_inc(S["sdbg"], 16)

            @block.scalar
            def _(s):
                s.wait_ge(S["sb1"], 5)
                s.wait_ge(S["s0cs"], 48)
                for qc in range(NQC):
                    for j in range(8):
                        gi = 3 * NCH + qc * 8 + j
                        if gi < N_GT:
                            wsel, hc = gtiles[gi]
                            s.wait_ge(S["sgv"], gi)
                            s.dma_start(out=gst2[:, :],
                                        in_=gsrc[wsel][hc].rearrange("p a b -> p (a b)")
                                        if wsel < 2 else gsrc[wsel][hc]
                                        ).then_inc(S["sgd"], 16)
                    for j in range(2):
                        tc = 2 * qc + j
                        for fb in range(4):
                            nfb = tc * 4 + fb
                            s.wait_ge(S["s2m"], nfb + 1)
                            for h in range(4):
                                s.activation(sqd[:, :],
                                             ps_q[:, nfb % 2, h * 128:(h + 1) * 128],
                                             AF.Square,
                                             accum_out=qssq[:, tc % 2, fb * 4 + h:fb * 4 + h + 1])
                            s.activation(qrstd[:, tc % 2, fb * 4:(fb + 1) * 4],
                                         qssq[:, tc % 2, fb * 4:(fb + 1) * 4],
                                         AF.Ln, scale=1.0 / DH, bias=cns[:, 1:2])
                            s.activation(qrstd[:, tc % 2, fb * 4:(fb + 1) * 4],
                                         qrstd[:, tc % 2, fb * 4:(fb + 1) * 4],
                                         AF.Exp, scale=-0.5).then_inc(S["s2sr"], 1)
                    kts = kts_of(qc)
                    npair = pairs_in(qc)
                    for h in range(H):
                        hh = qc * H + h
                        if hh >= 2:
                            s.wait_ge(S["s2c"], hh - 1)
                        for p in range(npair):
                            gp = gp_base(qc) + h * npair + p
                            s.wait_ge(S["s2s"], gp + 1)
                            if qc <= 3:
                                for kk in range(2):
                                    pos = 2 * p + kk
                                    lkt = 2 * qc + kts[pos]
                                    ins = s.activation(att_sb[:, hh % 2, pos],
                                                       ps_s[:, gp % 2, kk * QC:(kk + 1) * QC],
                                                       AF.Exp, scale=SCALE,
                                                       bias=kbias_sb[:, lkt:lkt + 1])
                                ins.then_inc(S["s2e"], 1)
                            else:
                                s.activation(
                                    att_sb.rearrange("p s k t -> p s (k t)")[:, hh % 2,
                                                                             2 * p * QC:(2 * p + 2) * QC],
                                    ps_s[:, gp % 2, :], AF.Exp, scale=SCALE
                                ).then_inc(S["s2e"], 1)
                        s.wait_ge(S["s2c"], hh + 1)
                        s.activation(lntmp[:, :], ps_yb[:, hh % 2, QC:2 * QC], AF.Ln)
                        s.activation(rdb[:, hh % 2], lntmp[:, :], AF.Exp, scale=-1.0
                                     ).then_inc(S["s2bc"], 1)

            @block.vector
            def _(v):
                v.wait_ge(S["sb1"], 5)
                v.wait_ge(S["s0cs"], 48)
                for qc in range(NQC):
                    for j in range(8):
                        gi = 3 * NCH + qc * 8 + j
                        if gi < N_GT:
                            wsel, hc = gtiles[gi]
                            v.wait_ge(S["sgd"], 16 * (gi + 1))
                            v.reduce_sum(gpart[:, gi:gi + 1], gst2[:, :], axis=AX.X,
                                         apply_absolute_value=True).then_inc(S["sgv"], 1)
                    for j in range(2):
                        tc = 2 * qc + j
                        for fb in range(4):
                            nfb = tc * 4 + fb
                            v.wait_ge(S["s2sr"], nfb + 1)
                            for h in range(4):
                                ins = v.tensor_scalar(qn_bf[:, tc % 2, fb * 4 + h],
                                                      ps_q[:, nfb % 2, h * 128:(h + 1) * 128],
                                                      scalar1=qrstd[:, tc % 2, fb * 4 + h:fb * 4 + h + 1],
                                                      scalar2=None, op0=ALU.mult)
                            ins.then_inc(S["s2qn"], 1)
                        v.wait_ge(S["s2cs"], 32 * (tc + 1))
                        if tc >= 2:
                            v.wait_ge(S["s2tt"], (tc - 1) * 4)
                        qn = qn_bf[:, tc % 2].rearrange("p h (half j) -> p h half j", half=2)
                        qr = qr_bf[:, tc % 2].rearrange("p h (half j) -> p h half j", half=2)
                        cq = cosq2[:, tc % 2].rearrange("p (h j) -> p h j", h=H)
                        sq = sinq2[:, tc % 2].rearrange("p (h j) -> p h j", h=H)
                        v.tensor_mul(qtmp[:, :, :], qn[:, :, 1, :], sq)
                        v.tensor_mul(qr[:, :, 0, :], qn[:, :, 0, :], cq)
                        v.tensor_sub(qr[:, :, 0, :], qr[:, :, 0, :], qtmp[:, :, :])
                        v.tensor_mul(qtmp[:, :, :], qn[:, :, 1, :], cq)
                        v.tensor_mul(qr[:, :, 1, :], qn[:, :, 0, :], sq)
                        v.tensor_add(qr[:, :, 1, :], qr[:, :, 1, :], qtmp[:, :, :]
                                     ).then_inc(S["s2rp"], 1)
                        for grp in range(4):
                            v.wait_ge(S["s2tt"], tc * 4 + grp + 1)
                            v.tensor_copy(
                                qf_sb[:, qc % 2, grp * 4:(grp + 1) * 4, j * 128:(j + 1) * 128],
                                ps_t2_l[(tc * 4 + grp) % 2][:, :, :]).then_inc(S["s2te"], 1)
                    att3 = att_sb.rearrange("p s k t -> p s (k t)")
                    npair = pairs_in(qc)
                    nt = len(kts_of(qc))
                    for h in range(H):
                        hh = qc * H + h
                        gpb = gp_base(qc) + h * npair
                        v.wait_ge(S["s2e"], gpb + 1)
                        v.tensor_mul(att3[:, hh % 2, 0:QC], att3[:, hh % 2, 0:QC],
                                     tri_sb[:, 0:QC]).then_inc(S["s2v"], 1)
                        v.wait_ge(S["s2e"], gpb + npair)
                        v.tensor_mul(att3[:, hh % 2, (nt - 2) * QC:nt * QC],
                                     att3[:, hh % 2, (nt - 2) * QC:nt * QC],
                                     tri_sb[:, 2 * QC:4 * QC]).then_inc(S["s2v"], 1)
                        v.wait_ge(S["s2bc"], hh + 1)
                        if h == 0 and qc >= 1:
                            v.wait_ge(S["s2yw"], 16 * qc)
                        v.tensor_mul(y_ch[:, h, :], ps_yb[:, hh % 2, 0:QC],
                                     rdb[:, hh % 2]).then_inc(S["s2yo"], 1)

            @block.tensor
            def _(t):
                t.wait_ge(S["sb1"], 5)
                t.wait_ge(S["s2wq"], 16 * 4)
                for qc in range(NQC):
                    for j in range(2):
                        tc = 2 * qc + j
                        t.wait_ge(S["s2x"], 16 * (tc + 1))
                        for fb in range(4):
                            nfb = tc * 4 + fb
                            if nfb >= 2:
                                t.wait_ge(S["s2qn"], nfb - 1)
                            for k in range(KCH):
                                ins = t.matmul(ps_q[:, nfb % 2, :], xn2[:, tc % 2, k],
                                               wq_bf[:, k, fb * 512:(fb + 1) * 512],
                                               start=(k == 0), stop=(k == KCH - 1),
                                               skip_group_check=True)
                            ins.then_inc(S["s2m"], 1)
                        t.wait_ge(S["s2rp"], tc + 1)
                        for grp in range(4):
                            ng = tc * 4 + grp
                            if ng >= 2:
                                t.wait_ge(S["s2te"], ng - 1)
                            if qc >= 2 and grp == 0 and j == 0:
                                t.wait_ge(S["s2s"], (qc - 1) * H * 5)
                            for h4 in range(4):
                                ins = t.transpose(ps_t2_l[ng % 2][:, h4],
                                                  qr_bf[:, tc % 2, grp * 4 + h4], idn_sb[:, :])
                            ins.then_inc(S["s2tt"], 1)
                    t.wait_ge(S["s2te"], 8 * qc + 8)
                    kts = kts_of(qc)
                    npair = pairs_in(qc)
                    nt = len(kts)
                    for h in range(H):
                        hh = qc * H + h
                        kvh = h // 4
                        gpb = gp_base(qc) + h * npair
                        for p in range(npair):
                            gp = gpb + p
                            if gp >= 2:
                                t.wait_ge(S["s2e"], gp - 1)
                            for kk in range(2):
                                pos = 2 * p + kk
                                lkt = 2 * qc + kts[pos]
                                ins = t.matmul(ps_s[:, gp % 2, kk * QC:(kk + 1) * QC],
                                               k_fm[:, kvh, lkt * 128:(lkt + 1) * 128],
                                               qf_sb[:, qc % 2, h, :],
                                               start=True, stop=True, skip_group_check=True)
                            ins.then_inc(S["s2s"], 1)
                        if hh >= 2:
                            t.wait_ge(S["s2yo"], hh - 1)
                        for pos in range(nt):
                            p = pos // 2
                            if p == 0 and pos == 0:
                                t.wait_ge(S["s2v"], hh * 2 + 1)
                            elif p == npair - 1:
                                t.wait_ge(S["s2v"], hh * 2 + 2)
                            else:
                                t.wait_ge(S["s2e"], gpb + p + 1)
                            lkt = 2 * qc + kts[pos]
                            t.matmul(ps_yb[:, hh % 2, 0:QC],
                                     v_tm[:, lkt, kvh * 128:(kvh + 1) * 128],
                                     att_sb[:, hh % 2, pos], start=(pos == 0),
                                     stop=(pos == nt - 1), skip_group_check=True)
                        if hh >= 2:
                            t.wait_ge(S["s2bc"], hh - 1)
                        for pos in range(nt):
                            ins = t.matmul(ps_yb[:, hh % 2, QC:2 * QC], ones_bf[:, :],
                                           att_sb[:, hh % 2, pos], start=(pos == 0),
                                           stop=(pos == nt - 1), skip_group_check=True)
                        ins.then_inc(S["s2c"], 1)

            @block.sync
            def _(s):
                s.nop().then_inc(S["sb2"], 1)

            @block.gpsimd
            def _(g):
                g.wait_ge(S["s2yw"], 16 * NQC)
                if dev:
                    g.wait_ge(S["sdbg"], 16 * (4 + NQC))
                g.nop().then_inc(S["sb2"], 1)

            @block.scalar
            def _(s):
                s.copy(dmy[:, 0:1], dmy[:, 1:2]).then_inc(S["sb2"], 1)

            @block.vector
            def _(v):
                v.wait_ge(S["sgv"], N_GT)
                v.tensor_copy(dmy[:, 0:1], dmy[:, 1:2]).then_inc(S["sb2"], 1)

            @block.tensor
            def _(t):
                t.wait_ge(S["s2te"], 8 * NQC)
                t.transpose(ps_t2_l[0][:, 0], idn_sb[:, :], idn_sb[:, :]
                            ).then_inc(S["sb2"], 1)


        kv_scope.close()

        with contextlib.ExitStack() as p3:
            e = p3.enter_context
            wo_bf = e(nc.sbuf_tensor("wo_bf", [128, KCH, D], BF16))
            y3 = e(nc.sbuf_tensor("y3", [128, 2, H, QC], BF16))
            r_bf = e(nc.sbuf_tensor("r_bf", [128, 2, D], BF16))
            ps_o3 = e(nc.psum_tensor("ps_o3", [128, 2, 4, 512], F32))

            @block.gpsimd
            def _(g):
                g.wait_ge(S["sb2"], 5)
                for part in range(4):
                    g.dma_start(out=wo_bf[:, part * 4:(part + 1) * 4, :],
                                in_=wow.rearrange("k p f -> p k f")[:, part * 4:(part + 1) * 4, :]
                                ).then_inc(S["s3wl"], 16)
                for tc in range(OWN // 128):
                    g.wait_ge(S["s3e"], tc + 1)
                    g.dma_start(out=rbs[tc], in_=r_bf[:, tc % 2]).then_inc(S["s3r"], 16)

            @block.sync
            def _(s):
                s.wait_ge(S["sb2"], 5)
                for qc in range(NQC):
                    if qc >= 2:
                        s.wait_ge(S["s3m"], (qc - 1) * 2)
                    s.dma_start(out=y3[:, qc % 2], in_=ybs[qc].rearrange("h p t -> p h t")
                                ).then_inc(S["s3y"], 16)

            @block.tensor
            def _(t):
                t.wait_ge(S["sb2"], 5)
                t.wait_ge(S["s3wl"], 16 * 4)
                for tc in range(OWN // 128):
                    qc, half = tc // 2, tc % 2
                    t.wait_ge(S["s3y"], 16 * (qc + 1))
                    if tc >= 2:
                        t.wait_ge(S["s3e"], tc - 1)
                    for k in range(KCH):
                        for db in range(4):
                            ins = t.matmul(ps_o3[:, tc % 2, db],
                                           y3[:, qc % 2, k, half * 128:(half + 1) * 128],
                                           wo_bf[:, k, db * 512:(db + 1) * 512],
                                           start=(k == 0), stop=(k == KCH - 1),
                                           skip_group_check=True)
                    ins.then_inc(S["s3m"], 1)

            @block.vector
            def _(v):
                v.wait_ge(S["sb2"], 5)
                for tc in range(OWN // 128):
                    v.wait_ge(S["s3m"], tc + 1)
                    if tc >= 2:
                        v.wait_ge(S["s3r"], 16 * (tc - 1))
                    v.tensor_copy(r_bf[:, tc % 2],
                                  ps_o3.rearrange("p s d f -> p s (d f)")[:, tc % 2]
                                  ).then_inc(S["s3e"], 1)

            @block.sync
            def _(s):
                s.nop().then_inc(S["sb3"], 1)

            @block.gpsimd
            def _(g):
                g.wait_ge(S["s3r"], 16 * (OWN // 128))
                g.nop().then_inc(S["sb3"], 1)

            @block.scalar
            def _(s):
                s.wait_ge(S["sb2"], 5)
                _load_act_set(s, nc, "sigmoid_and_others")
                s.copy(dmy[:, 0:1], dmy[:, 1:2]).then_inc(S["sb3"], 1)

            @block.vector
            def _(v):
                v.tensor_copy(dmy[:, 0:1], dmy[:, 1:2]).then_inc(S["sb3"], 1)

            @block.tensor
            def _(t):
                t.wait_ge(S["s3e"], OWN // 128)
                t.matmul(ps_o3[:, 0, 0, 0:128], idn_sb[:, :], idn_sb[:, :],
                         start=True, stop=True, skip_group_check=True
                         ).then_inc(S["sb3"], 1)


        with contextlib.ExitStack() as p4a:
            e = p4a.enter_context
            red4 = e(nc.sbuf_tensor("red4", [1, 3, HCH * 128], F32))
            gsum = e(nc.sbuf_tensor("gsum", [1, 4], F32))
            arow = e(nc.sbuf_tensor("arow", [1, 8], F32))

            @block.sync
            def _(s):
                s.wait_ge(S["sb3"], 5)
                s.dma_start(out=gsc_hbm, in_=gpart[:, :]).then_inc(S["s4gd"], 16)
                s.wait_ge(S["s4gd"], 16)
                for w in range(3):
                    s.dma_start(out=red4[:1, w],
                                in_=gsc_hbm[:, w * HCH:(w + 1) * HCH].rearrange("p t -> t p"),
                                allow_slow_non_contiguous=True).then_inc(S["s4gd"], 16)
                s.wait_ge(S["s4g"], 2)
                s.dma_start(out=arow_hbm[:1, :], in_=arow[:1, :]).then_inc(S["s4gd"], 16)
                s.wait_ge(S["s4gd"], 80)
                s.dma_start(out=a_sb[:, :], in_=arow_hbm[0:1, :].broadcast_to([128, 8])
                            ).then_inc(S["s4gd"], 16)

            @block.vector
            def _(v):
                v.wait_ge(S["s4gd"], 64)
                for w in range(3):
                    ins = v.reduce_sum(gsum[:1, w:w + 1], red4[:1, w], axis=AX.X)
                ins.then_inc(S["s4g"], 1)
                v.wait_ge(S["s4g"], 1)
                NW = float(HID * D)
                for w in range(3):
                    v.tensor_scalar(arow[:1, 2 * w:2 * w + 1], gsum[:1, w:w + 1],
                                    scalar1=0.5 / NW, scalar2=0.5 * Q_EPS,
                                    op0=ALU.mult, op1=ALU.add)
                    ins = v.tensor_scalar(arow[:1, 2 * w + 1:2 * w + 2], gsum[:1, w:w + 1],
                                          scalar1=-0.5 / NW, scalar2=-0.5 * Q_EPS,
                                          op0=ALU.mult, op1=ALU.add)
                ins.then_inc(S["s4g"], 1)

        with contextlib.ExitStack() as p4:
            e = p4.enter_context
            xn4 = e(nc.sbuf_tensor("xn4", [128, NTC, KCH, 128], BF16))
            hid_sb = e(nc.sbuf_tensor("hid_sb", [128, HCH, TB], BF16))
            wst = e(nc.sbuf_tensor("wst", [128, 2, 2, KCH * 128], F32))
            qw12 = e(nc.sbuf_tensor("qw12", [128, 2, 2, KCH, 128], BF16))
            s12 = e(nc.sbuf_tensor("s12", [128, 2, KCH * 128], BF16))
            wst3 = e(nc.sbuf_tensor("wst3", [128, 2, D], F32))
            qw3 = e(nc.sbuf_tensor("qw3", [128, 2, D], BF16))
            s3t = e(nc.sbuf_tensor("s3t", [128, D], BF16))
            sg_bf = e(nc.sbuf_tensor("sg_bf", [128, TB], BF16))
            r4 = e(nc.sbuf_tensor("r4", [128, 2, 1024], BF16))
            x4 = e(nc.sbuf_tensor("x4", [128, 2, 1024], F32))
            tmp4 = e(nc.sbuf_tensor("tmp4", [128, 2, 1024], F32))
            out4 = e(nc.sbuf_tensor("out4", [128, 2, 1024], F32))
            ps_u = e(nc.psum_tensor("ps_u", [128, 4, 1024], F32))

            def psx(slot, w):
                return ps_u[:, slot, w * 512:(w + 1) * 512]

            def qpath(hc):
                return "v" if hc % 3 == 0 else "s"

            def nsgn(hc):
                return sum(1 for x in range(hc + 1) if qpath(x) == "s")

            q12f = lambda sl, w: qw12.rearrange("p s w k t -> p s w (k t)")[:, sl, w]

            @block.scalar
            def _(s):
                s.wait_ge(S["sb3"], 5)
                s.wait_ge(S["s4gd"], 96)
                for b in range(NTB):
                    for hc in range(HCH):
                        ghc = b * HCH + hc
                        for w in range(2):
                            if b == 0:
                                if hc >= 2:
                                    s.wait_ge(S["s4q"], (hc - 2) * 2 + 2)
                                src = w1w if w == 0 else w2w
                                s.dma_start(out=wst[:, hc % 2, w],
                                            in_=src[hc].rearrange("p a b -> p (a b)")
                                            ).then_inc(S["s4w"], 16)
                            else:
                                if ghc >= 2:
                                    s.wait_ge(S["s4ht"], ghc - 1)
                                s.wait_ge(S["s4qws"], 16 * (2 * hc + 2))
                                src = qw1s if w == 0 else qw2s
                                s.dma_start(out=q12f(hc % 2, w), in_=src[hc]
                                            ).then_inc(S["s4w"], 16)
                        if b == 0 and qpath(hc) == "s":
                            s.wait_ge(S["s4w"], 16 * (2 * hc + 2))
                            s.wait_ge(S["s4q"], 2 * hc)
                            if hc >= 2:
                                s.wait_ge(S["s4qws"], 16 * (2 * hc - 2))
                            if hc >= 2:
                                s.wait_ge(S["s4ht"], hc - 1)
                            for w in range(2):
                                s.activation(s12[:, w], wst[:, hc % 2, w], AF.Sign,
                                             bias=a_sb[:, 2 * w + 1:2 * w + 2])
                                ins = s.activation(q12f(hc % 2, w), wst[:, hc % 2, w], AF.Sign,
                                                   bias=a_sb[:, 2 * w:2 * w + 1])
                            ins.then_inc(S["s4sgn"], 1)
                        s.wait_ge(S["s4ht"], ghc + 1)
                        s.wait_ge(S["s4h"], ghc)
                        s.activation(sg_bf[:, :], psx(ghc % 2, 0), AF.Sigmoid
                                     ).then_inc(S["s4sg"], 1)

            @block.vector
            def _(v):
                v.wait_ge(S["sb3"], 5)
                v.wait_ge(S["s4gd"], 96)
                for b in range(NTB):
                    for hc in range(HCH):
                        ghc = b * HCH + hc
                        if b == 0:
                            if qpath(hc) == "v":
                                v.wait_ge(S["s4w"], 16 * (2 * hc + 2))
                                if hc >= 2:
                                    v.wait_ge(S["s4ht"], hc - 1)
                                    v.wait_ge(S["s4qws"], 16 * (2 * hc - 2))
                                for w in range(2):
                                    v.tensor_scalar(s12[:, w], wst[:, hc % 2, w],
                                                    scalar1=a_sb[:, 2 * w:2 * w + 1],
                                                    scalar2=None, op0=ALU.is_gt)
                                    v.tensor_scalar(q12f(hc % 2, w), wst[:, hc % 2, w],
                                                    scalar1=-1.0,
                                                    scalar2=a_sb[:, 2 * w:2 * w + 1],
                                                    op0=ALU.mult, op1=ALU.is_gt)
                                    ins = v.tensor_sub(q12f(hc % 2, w), s12[:, w], q12f(hc % 2, w))
                                ins.then_inc(S["s4q"], 2)
                            else:
                                v.wait_ge(S["s4sgn"], nsgn(hc))
                                if hc >= 2:
                                    v.wait_ge(S["s4ht"], hc - 1)
                                    v.wait_ge(S["s4qws"], 16 * (2 * hc - 2))
                                for w in range(2):
                                    v.tensor_add(q12f(hc % 2, w), q12f(hc % 2, w), s12[:, w])
                                    ins = v.tensor_scalar(q12f(hc % 2, w), q12f(hc % 2, w),
                                                          scalar1=0.5, scalar2=None, op0=ALU.mult)
                                ins.then_inc(S["s4q"], 2)
                        v.wait_ge(S["s4sg"], ghc + 1)
                        v.tensor_mul(hid_sb[:, hc, :], sg_bf[:, :], psx(ghc % 2, 1)
                                     ).then_inc(S["s4h"], 1)
                    for dh in range(2):
                        if b == 0 and dh == 0:
                            for hc in range(HCH):
                                v.wait_ge(S["s4w3l"], 16 * (hc + 1))
                                if hc >= 2:
                                    v.wait_ge(S["s4w3c"], hc - 1)
                                    v.wait_ge(S["s4qws"], 16 * (2 * HCH + hc - 1))
                                v.tensor_scalar(s3t[:, :], wst3[:, hc % 2],
                                                scalar1=a_sb[:, 4:5], scalar2=None,
                                                op0=ALU.is_gt)
                                v.tensor_scalar(qw3[:, hc % 2], wst3[:, hc % 2],
                                                scalar1=-1.0, scalar2=a_sb[:, 4:5],
                                                op0=ALU.mult, op1=ALU.is_gt)
                                v.tensor_sub(qw3[:, hc % 2], s3t[:, :], qw3[:, hc % 2]
                                             ).then_inc(S["s4w3q"], 1)
                        for tc in range(NTC):
                            idx = (b * 2 + dh) * NTC + tc
                            v.wait_ge(S["s4o"], idx + 1)
                            v.wait_ge(S["s4r"], 16 * (idx + 1))
                            v.wait_ge(S["s4x"], 16 * (idx + 1))
                            if idx >= 2:
                                v.wait_ge(S["s4out"], 16 * (idx - 1))
                            sl = idx % 2
                            v.tensor_add(tmp4[:, sl], r4[:, sl], x4[:, sl]
                                         ).then_inc(S["s4tt"], 1)
                            v.wait_ge(S["s4tt"], idx + 1)
                            v.tensor_add(out4[:, sl], ps_u[:, tc, :], tmp4[:, sl]
                                         ).then_inc(S["s4ev"], 1)

            @block.sync
            def _(s):
                s.wait_ge(S["sb3"], 5)
                for b in range(NTB):
                    if b >= 1:
                        s.wait_ge(S["s4h"], HCH * b)
                    for tc in range(NTC):
                        s.dma_start(out=xn4.rearrange("p f k t -> p f (k t)")[:, tc],
                                    in_=xns[OC0 + b * NTC + tc]).then_inc(S["s4xn"], 16)
                    for dh in range(2):
                        for tc in range(NTC):
                            idx = (b * 2 + dh) * NTC + tc
                            row = b * TB + tc * 128
                            if idx >= 2:
                                s.wait_ge(S["s4ev"], idx - 1)
                            s.dma_start(out=r4[:, idx % 2],
                                        in_=rbs[b * NTC + tc][:, dh * 1024:(dh + 1) * 1024]
                                        ).then_inc(S["s4r"], 16)
                            s.dma_start(out=x4[:, idx % 2],
                                        in_=xin[HALO + row:HALO + row + 128,
                                                dh * 1024:(dh + 1) * 1024]
                                        ).then_inc(S["s4x"], 16)
                            s.wait_ge(S["s4ev"], idx + 1)
                            s.dma_start(out=out[row:row + 128, dh * 1024:(dh + 1) * 1024],
                                        in_=out4[:, idx % 2]).then_inc(S["s4out"], 16)
                s.wait_ge(S["s4out"], 16 * NTB * 2 * NTC)

            @block.gpsimd
            def _(g):
                g.wait_ge(S["sb3"], 5)
                for hc in range(HCH):
                    g.wait_ge(S["s4q"], 2 * hc + 2)
                    g.dma_start(out=qw1s[hc], in_=q12f(hc % 2, 0)).then_inc(S["s4qws"], 16)
                    g.dma_start(out=qw2s[hc], in_=q12f(hc % 2, 1)).then_inc(S["s4qws"], 16)
                for hc in range(HCH):
                    if hc >= 2:
                        g.wait_ge(S["s4w3q"], hc - 1)
                        g.dma_start(out=qw3s[hc - 2], in_=qw3[:, hc % 2]
                                    ).then_inc(S["s4qws"], 16)
                    g.dma_start(out=wst3[:, hc % 2], in_=w3w[hc]).then_inc(S["s4w3l"], 16)
                for hc in (HCH - 2, HCH - 1):
                    g.wait_ge(S["s4w3q"], hc + 1)
                    g.dma_start(out=qw3s[hc], in_=qw3[:, hc % 2]).then_inc(S["s4qws"], 16)
                for si in range(1, 8):
                    for hc in range(HCH):
                        gw3 = si * HCH + hc
                        g.wait_ge(S["s4w3c"], gw3 - 1)
                        g.wait_ge(S["s4qws"], 16 * (2 * HCH + hc + 1))
                        g.dma_start(out=qw3[:, hc % 2], in_=qw3s[hc]).then_inc(S["s4w3l"], 16)
                if dev:
                    g.wait_ge(S["s4h"], HCH * NTB)
                    g.dma_start(out=hiddbg, in_=hid_sb[:, :, :]).then_inc(S["sdbg"], 16)

            @block.tensor
            def _(t):
                t.wait_ge(S["sb3"], 5)
                t.wait_ge(S["s4gd"], 96)
                for b in range(NTB):
                    t.wait_ge(S["s4xn"], 16 * NTC * (b + 1))
                    if b >= 1:
                        t.wait_ge(S["s4ev"], b * 2 * NTC)
                    for hc in range(HCH):
                        ghc = b * HCH + hc
                        if b == 0:
                            t.wait_ge(S["s4q"], 2 * hc + 2)
                        else:
                            t.wait_ge(S["s4w"], 16 * (2 * ghc + 2))
                        if ghc >= 2:
                            t.wait_ge(S["s4sg"], ghc - 1)
                            t.wait_ge(S["s4h"], ghc - 1)
                        for w in range(2):
                            for k in range(KCH):
                                ins = t.matmul(psx(ghc % 2, w), qw12[:, hc % 2, w, k],
                                               xn4[:, :, k, :],
                                               start=(k == 0), stop=(k == KCH - 1),
                                               skip_group_check=True)
                        ins.then_inc(S["s4ht"], 1)
                    for dh in range(2):
                        si = b * 2 + dh
                        for hc in range(HCH):
                            gw3 = si * HCH + hc
                            if si == 0:
                                t.wait_ge(S["s4w3q"], hc + 1)
                            else:
                                t.wait_ge(S["s4w3l"], 16 * (gw3 + 1))
                            if hc == 0:
                                t.wait_ge(S["s4h"], HCH * (b + 1))
                                if si >= 1:
                                    t.wait_ge(S["s4ev"], si * NTC)
                            for tc in range(NTC):
                                for db in range(2):
                                    ins = t.matmul(
                                        ps_u[:, tc, db * 512:(db + 1) * 512],
                                        hid_sb[:, hc, tc * 128:(tc + 1) * 128],
                                        qw3[:, hc % 2, dh * 1024 + db * 512:dh * 1024 + (db + 1) * 512],
                                        start=(hc == 0), stop=(hc == HCH - 1),
                                        skip_group_check=True)
                                if hc == HCH - 1 and tc < NTC - 1:
                                    ins.then_inc(S["s4o"], 1)
                            if hc == HCH - 1:
                                ins.then_inc(S["s4o"], 1)
                                t.wait_ge(S["s4o"], (b * 2 + dh) * NTC + NTC)
                                t.nop().then_inc(S["s4w3c"], 1)
                            else:
                                ins.then_inc(S["s4w3c"], 1)

    return nc


_CACHE = {}


def _host_prep(L0_attn_x, L0_attn_wq, L0_attn_wk, L0_attn_wv, L0_attn_wo,
               L0_ffn_w1, L0_ffn_w2, L0_ffn_w3):
    bf = ml_dtypes.bfloat16
    X2 = np.ascontiguousarray(np.asarray(L0_attn_x, dtype=np.float32).reshape(B * T, D))

    perm = np.concatenate([h * DH + np.concatenate([np.arange(0, DH, 2), np.arange(1, DH, 2)])
                           for h in range(H)])
    permk = np.concatenate([h * DH + np.concatenate([np.arange(0, DH, 2), np.arange(1, DH, 2)])
                            for h in range(KV)])
    wq = np.asarray(L0_attn_wq, np.float32)[perm]
    wk = np.asarray(L0_attn_wk, np.float32)[permk]
    wv = np.asarray(L0_attn_wv, np.float32)
    qkv = np.ascontiguousarray(
        np.concatenate([wq, wk, wv], axis=0).T.reshape(KCH, 128, HD + 2 * KVD))
    wow_np = np.ascontiguousarray(np.asarray(L0_attn_wo, np.float32).T.reshape(KCH, 128, D))
    w1w_np = np.ascontiguousarray(
        np.asarray(L0_ffn_w1, np.float32).reshape(HCH, 128, KCH, 128).transpose(0, 3, 2, 1))
    w2w_np = np.ascontiguousarray(
        np.asarray(L0_ffn_w2, np.float32).reshape(HCH, 128, KCH, 128).transpose(0, 3, 2, 1))
    w3w_np = np.ascontiguousarray(np.asarray(L0_ffn_w3, np.float32).T.reshape(HCH, 128, D))

    inv = (1.0 / 10000.0 ** (np.arange(0, DH, 2, dtype=np.float64) / DH)).astype(np.float32)
    kk = np.arange(128)[:, None]
    qq = np.arange(QC)[None, :]
    tri_np = np.zeros((128, 4, QC), dtype=bf)
    tri_np[:, 0] = ((kk >= 1) | (qq < 0)).astype(bf)
    tri_np[:, 1] = 1.0
    tri_np[:, 2] = (kk <= qq).astype(bf)
    tri_np[:, 3] = (128 + kk <= qq).astype(bf)
    idn_np = np.eye(128, dtype=bf)

    in_maps = []
    for c in range(8):
        seq, slot = divmod(c, 4)
        pos0 = slot * OWN
        r0 = seq * T + pos0
        ownx = X2[r0:r0 + OWN]
        halo = np.zeros((HALO, D), np.float32) if pos0 == 0 else X2[r0 - HALO:r0]
        xin_np = np.ascontiguousarray(np.concatenate([halo, ownx], axis=0))

        pos = np.maximum(pos0 - HALO + np.arange(TOT), 0).astype(np.float32)
        ang = (pos[:, None] * inv[None, :]).astype(np.float32)
        cos_t = np.cos(ang.astype(np.float64)).astype(np.float32)
        sin_t = np.sin(ang.astype(np.float64)).astype(np.float32)
        cosq_np = np.ascontiguousarray(np.tile(cos_t, (1, H)).astype(bf))
        sinq_np = np.ascontiguousarray(np.tile(sin_t, (1, H)).astype(bf))

        kb = np.zeros((128, NCH), np.float32)
        gkey = pos0 - HALO + (np.arange(NCH)[None, :] * 128 + np.arange(128)[:, None])
        kb[gkey < 0] = NEG

        in_maps.append({
            "xin": xin_np, "qkvw": qkv, "wow": wow_np, "w1w": w1w_np,
            "w2w": w2w_np, "w3w": w3w_np, "cosq": cosq_np, "sinq": sinq_np,
            "tri": tri_np, "kbias": kb, "idn": idn_np,
        })
    return in_maps


def kernel(**inputs):
    in_maps = _host_prep(**inputs)
    if "nc" not in _CACHE:
        _CACHE["nc"] = build_program()
    res = run_bass_kernel_spmd(_CACHE["nc"], in_maps, core_ids=list(range(8)))
    Y = np.empty((B * T, D), dtype=np.float32)
    for c in range(8):
        seq, slot = divmod(c, 4)
        r0 = seq * T + slot * OWN
        Y[r0:r0 + OWN] = res.results[c]["out"]
    return Y.reshape(B, T, D)
